# version 15 (final)
# speedup vs baseline: 2.1334x; 1.0162x over previous
import os
import numpy as np
import ml_dtypes
import concourse.bass as bass
import concourse.mybir as mybir
from concourse.bass_utils import run_bass_kernel_spmd

_DETECT_RACES = os.environ.get("KDEV_RACE", "0") == "1"


def _new_nc():
    return bass.Bass("TRN2", target_bir_lowering=False, debug=False,
                     detect_race_conditions=_DETECT_RACES)

F32 = mybir.dt.float32
BF16 = mybir.dt.bfloat16
U32 = mybir.dt.uint32
I32 = mybir.dt.int32
AF = mybir.ActivationFunctionType
ALU = mybir.AluOpType
AX = mybir.AxisListType.X
BF = ml_dtypes.bfloat16

DMA_ENGINES = ("act", "pool", "sp")
ALL_ENGINES = ("pe", "act", "dve", "pool", "sp")


class Buf:

    def __init__(self, prog, name, handle, kind):
        self.prog, self.name, self.t, self.kind = prog, name, handle, kind
        self.writer = {}
        self.readers = {}
        self.init_deps = []

    def __repr__(self):
        return f"Buf({self.name})"


DMA_SEM_POOL = 10


class _Op:
    __slots__ = ("oid", "eng", "emit", "dma", "needs_inc", "deps", "tag",
                 "dma_sem", "dma_val", "dma_n")

    def __init__(self, oid, eng, emit, dma, deps, tag):
        self.oid, self.eng, self.emit, self.dma = oid, eng, emit, dma
        self.deps, self.tag = deps, tag
        self.needs_inc = False
        self.dma_sem = None
        self.dma_val = 0
        self.dma_n = 1


class Prog:
    def __init__(self, nc, n_exchange_flags=0):
        self.nc = nc
        self.ops = []
        self.sbuf_base = (nc.sbuf_base + 4096 + 63) // 64 * 64
        self.sbuf_limit = nc.sbuf_top
        self.sbuf_free = [(self.sbuf_base, self.sbuf_limit)]
        self.sbuf_hist = []
        self.sbuf_peak = 0
        self._uid = 0
        self._psum_bufs = None
        self.n_flags = n_exchange_flags
        self.shf = None
        self.sems = None
        self.msem = None
        self._preg = None
        self._parity_sb = None

    def uid(self):
        self._uid += 1
        return self._uid

    def sbuf(self, name, shape, dtype, align=64, side="lo"):
        nbytes = int(np.prod(shape[1:])) * mybir.dt.size(dtype)
        nbytes = (nbytes + align - 1) // align * align
        best = None
        if side == "hi":
            for i in range(len(self.sbuf_free) - 1, -1, -1):
                s, e = self.sbuf_free[i]
                if e - s >= nbytes:
                    best = (i,)
                    break
        else:
            for i, (s, e) in enumerate(self.sbuf_free):
                if e - s >= nbytes:
                    best = (i,)
                    break
        assert best is not None, \
            f"SBUF OOM for {name}: need {nbytes}B/partition, free={self.sbuf_free}"
        i = best[0]
        s, e = self.sbuf_free[i]
        if side == "hi":
            a = e - nbytes
            if a == s:
                self.sbuf_free.pop(i)
            else:
                self.sbuf_free[i] = (s, a)
        else:
            a = s
            if s + nbytes == e:
                self.sbuf_free.pop(i)
            else:
                self.sbuf_free[i] = (s + nbytes, e)
        self.sbuf_peak = max(self.sbuf_peak, a + nbytes)
        h = self.nc.alloc_sbuf_tensor_at(f"{name}_{self.uid()}", shape, dtype, offset=a)
        b = Buf(self, name, h, "sbuf")
        b._range = (a, a + nbytes)
        deps = set()
        for (hs, he, dops) in self.sbuf_hist:
            if hs < a + nbytes and a < he:
                deps.update(dops)
        b.init_deps = sorted(deps)
        return b

    def free(self, b):
        s, e = b._range
        dops = set(b.writer.values())
        for rs in b.readers.values():
            dops.update(rs)
        self.sbuf_hist.append((s, e, dops))
        if len(self.sbuf_hist) > 512:
            self.sbuf_hist = self.sbuf_hist[-512:]
        self.sbuf_free.append((s, e))
        self.sbuf_free.sort()
        merged = []
        for seg in self.sbuf_free:
            if merged and merged[-1][1] == seg[0]:
                merged[-1] = (merged[-1][0], seg[1])
            else:
                merged.append(seg)
        self.sbuf_free = merged

    def psum_init(self):
        bufs = []
        for i in range(6):
            h = self.nc.alloc_psum_tensor(f"bank{i}", [128, 512], F32)
            bufs.append(Buf(self, f"bank{i}", h, "psum"))
        for i in range(6, 8):
            h = self.nc.alloc_psum_tensor(f"bank{i}", [128, 1024], BF16)
            bufs.append(Buf(self, f"bank{i}", h, "psum"))
        self._psum_bufs = bufs
        return bufs

    def dram(self, name, shape, dtype, shared=False):
        h = self.nc.dram_tensor(f"{name}_{self.uid()}", shape, dtype,
                                addr_space="Shared" if shared else "Local")
        return Buf(self, name, h, "dram")

    def wrap(self, handle, name, kind):
        return Buf(self, name, handle, kind)

    def _wconf(self, buf, key):
        if key is None:
            return list(buf.writer.values())
        out = []
        if key in buf.writer:
            out.append(buf.writer[key])
        if None in buf.writer:
            out.append(buf.writer[None])
        return out

    def _rconf(self, buf, key):
        if key is None:
            return [r for rs in buf.readers.values() for r in rs]
        return list(buf.readers.get(key, ())) + list(buf.readers.get(None, ()))

    def op(self, eng, emit, reads=(), writes=(), dma=False, tag="", dma_n=1):
        oid = len(self.ops)
        deps = set()

        def norm(x):
            return x if isinstance(x, tuple) else (x, None)

        for r in reads:
            b, k = norm(r)
            deps.update(self._wconf(b, k))
        for w in writes:
            b, k = norm(w)
            deps.update(self._wconf(b, k))
            deps.update(self._rconf(b, k))
            if b.init_deps:
                deps.update(b.init_deps)
        for r in reads:
            b, k = norm(r)
            b.readers.setdefault(k, []).append(oid)
        for w in writes:
            b, k = norm(w)
            if k is None:
                b.init_deps = []
                b.writer = {None: oid}
                b.readers = {}
            else:
                b.writer[k] = oid
                b.readers.pop(k, None)
                b.readers.pop(None, None)
        deps.discard(oid)
        o = _Op(oid, eng, emit, dma, sorted(deps), tag)
        if dma:
            if not hasattr(self, "_dma_rot"):
                self._dma_rot = {}
                self._dma_uses = {}
            i = self._dma_rot.get(eng, 0)
            self._dma_rot[eng] = i + 1
            sname = f"d_{eng}_{i % DMA_SEM_POOL}"
            self._dma_uses[sname] = self._dma_uses.get(sname, 0) + dma_n
            o.dma_sem = sname
            o.dma_val = 16 * self._dma_uses[sname]
            o.dma_n = dma_n
        self.ops.append(o)
        return oid

    def emit(self):
        nc = self.nc
        for o in self.ops:
            for d in o.deps:
                self.ops[d].needs_inc = True
        cval = {}
        ccount = {e: 0 for e in ALL_ENGINES}
        for o in self.ops:
            if o.dma:
                cval[o.oid] = (o.dma_sem, o.dma_val)
            elif o.needs_inc:
                ccount[o.eng] += 1
                cval[o.oid] = (f"c_{o.eng}", ccount[o.eng])

        sems = {}
        for e in ALL_ENGINES:
            sems[f"c_{e}"] = nc.alloc_semaphore(f"c_{e}")
        for e in DMA_ENGINES:
            for i in range(DMA_SEM_POOL):
                n = f"d_{e}_{i}"
                if getattr(self, "_dma_uses", {}).get(n):
                    sems[n] = nc.alloc_semaphore(n)
        self.sems = sems
        self.msem = nc.monotonic_semaphore(0, engine=nc.gpsimd)

        per_eng = {e: [] for e in ALL_ENGINES}
        for o in self.ops:
            per_eng[o.eng].append(o)
        watermark = {e: {} for e in ALL_ENGINES}

        def run_engine(ename, eng):
            wm = watermark[ename]
            for o in per_eng[ename]:
                prev = o.dma_val - 16 * o.dma_n if o.dma else 0
                if o.dma and prev > 0 and wm.get(o.dma_sem, 0) < prev:
                    eng.wait_ge(self.sems[o.dma_sem], prev)
                    wm[o.dma_sem] = prev
                for d in o.deps:
                    if d not in cval:
                        continue
                    if ename == "pe" and self.ops[d].eng == "pe":
                        continue
                    sname, v = cval[d]
                    if wm.get(sname, 0) >= v:
                        continue
                    eng.wait_ge(self.sems[sname], v)
                    wm[sname] = v
                inst = o.emit(eng)
                if o.oid in cval:
                    sname, _ = cval[o.oid]
                    amt = 16 if o.dma else 1
                    if inst == "self":
                        pass
                    elif inst is None:
                        eng.sem_inc(self.sems[sname], amt)
                    else:
                        inst.then_inc(self.sems[sname], amt)

        with nc.Block() as block:
            block.tensor(lambda e: run_engine("pe", e))
            block.scalar(lambda e: run_engine("act", e))
            block.vector(lambda e: run_engine("dve", e))
            block.gpsimd(lambda e: run_engine("pool", e))
            block.sync(lambda e: run_engine("sp", e))

    def constf(self, value):
        if not hasattr(self, "_consts"):
            self._consts = {}
        if value not in self._consts:
            b = self.sbuf(f"const_{len(self._consts)}", [128, 1], F32, side="hi")
            self.op("pool", lambda e, b=b, v=value: e.memset(b.t[:, :], float(v)),
                    (), [b], tag="constf")
            self._consts[value] = b
        return self._consts[value]

    def dma(self, eng, out_ap, in_ap, reads, writes, tag="dma", accum=None):
        def f(e):
            kw = {}
            if accum is not None:
                kw["accum_op"] = accum
            return e.dma_start(out=out_ap, in_=in_ap, **kw)
        return self.op(eng, f, reads, writes, dma=True, tag=tag)

    def mm(self, groups, reads, writes, tag="mm"):
        def f(e):
            inst = None
            for (o, l, r, st, sp, tp) in groups:
                kw = {"tile_position": tp} if tp is not None else {}
                inst = e.matmul(o, l, r, start=st, stop=sp, **kw)
            return inst
        return self.op("pe", f, reads, writes, tag=tag)

    def tr(self, pairs, ident_buf, reads, writes, tag="tr", ksize=128):
        def f(e):
            inst = None
            for (o, i) in pairs:
                inst = e.transpose(o, i, ident_buf.t[0:ksize, 0:ksize])
            return inst
        return self.op("pe", f, list(reads) + [ident_buf], writes, tag=tag)

    def act(self, out_ap, in_ap, func, reads, writes, bias=0.0, scale=1.0,
            accum_out=None, tag="act"):
        reads = list(reads)
        if isinstance(bias, float) and bias != 0.0:
            cb = self.constf(bias)
            bias = cb.t[:, 0:1]
            reads.append(cb)
        if isinstance(scale, float) and scale not in (1.0,):
            cs = self.constf(scale)
            scale = cs.t[:, 0:1]
            reads.append(cs)

        def f(e):
            return e.activation(out_ap, in_ap, func, bias=bias, scale=scale,
                                accum_out=accum_out)
        return self.op("act", f, reads, writes, tag=tag)

    def dve(self, emit, reads, writes, tag="dve"):
        return self.op("dve", emit, reads, writes, tag=tag)

    def pool(self, emit, reads, writes, tag="pool"):
        return self.op("pool", emit, reads, writes, tag=tag)

    def setup_parity(self, parity_dram_buf, nonce_dram_buf, n_flags):
        self.n_flags = n_flags
        self.shf = self.dram("shf", [n_flags, 2], U32, shared=True)
        self._parity_sb = self.sbuf("parity_sb", [1, 1], U32)
        self._nonce_sb = self.sbuf("nonce_sb", [1, n_flags], U32)
        self._pollcell = self.sbuf("pollcell", [1, 2], U32)
        self.dma("pool", self._parity_sb.t[0:1, 0:1], parity_dram_buf.t.ap()[0:1, 0:1],
                 [parity_dram_buf], [self._parity_sb], tag="par_ld")
        self.dma("pool", self._nonce_sb.t[0:1, :], nonce_dram_buf.t.ap()[0:1, :],
                 [nonce_dram_buf], [self._nonce_sb], tag="nonce_ld")

        self._preg = {}
        for peng in ("pool", "sp"):
            def f(e, peng=peng):
                self._preg[peng] = e.alloc_register(f"preg_{peng}")
                e.reg_load(self._preg[peng], self._parity_sb.t[0:1, 0:1])
                return None
            self.op(peng, f, [self._parity_sb], (), tag=f"par_reg_{peng}")

    def pdma(self, out_aps_by_par, in_aps_by_par, reads, writes, tag="pdma",
             n=1, eng="sp"):
        prog = self
        cell = {}

        def f(e):
            sem = prog.sems[cell["sem"]]

            def go(par):
                outs = out_aps_by_par(par)
                ins = in_aps_by_par(par)
                if n == 1:
                    outs, ins = [outs], [ins]
                for o2, i2 in zip(outs, ins):
                    e.dma_start(out=o2, in_=i2).then_inc(sem, 16)
            with e.If_eq(prog._preg[eng], 0):
                go(0)
            with e.Else():
                go(1)
            return "self"
        oid = self.op(eng, f, reads, writes, dma=True, tag=tag, dma_n=n)
        cell["sem"] = self.ops[oid].dma_sem
        return oid

    def send_flag(self, flag_idx, reads, tag="flag", eng="sp"):
        prog = self
        cell = {}

        def f(e):
            sem = prog.sems[cell["sem"]]
            with e.If_eq(prog._preg[eng], 0):
                e.dma_start(out=prog.shf.t.ap()[flag_idx:flag_idx + 1, 0:1],
                            in_=prog._nonce_sb.t[0:1, flag_idx:flag_idx + 1]).then_inc(sem, 16)
            with e.Else():
                e.dma_start(out=prog.shf.t.ap()[flag_idx:flag_idx + 1, 1:2],
                            in_=prog._nonce_sb.t[0:1, flag_idx:flag_idx + 1]).then_inc(sem, 16)
            return "self"
        oid = self.op(eng, f, list(reads) + [self._nonce_sb],
                      [(self.shf, flag_idx)], dma=True, tag=tag)
        cell["sem"] = self.ops[oid].dma_sem
        return oid

    def poll_flag(self, flag_idx, tag="poll"):
        prog = self
        cell = self._pollcell

        def f(e):
            f0 = e.alloc_register(f"pf0_{flag_idx}")
            f1 = e.alloc_register(f"pf1_{flag_idx}")
            want = e.alloc_register(f"pw_{flag_idx}")
            dif = e.alloc_register(f"pd_{flag_idx}")
            d1 = e.alloc_register(f"pe_{flag_idx}")
            e.reg_load(want, prog._nonce_sb.t[0:1, flag_idx:flag_idx + 1])
            e.reg_mov(dif, 1)
            with e.While(dif):
                e.dma_start(out=cell.t[0:1, :],
                            in_=prog.shf.t.ap()[flag_idx:flag_idx + 1, :]) \
                    .then_inc(prog.msem.sem(), 16)
                prog.msem.inc_expected(16)
                prog.msem.wait()
                e.reg_load(f0, cell.t[0:1, 0:1])
                e.reg_load(f1, cell.t[0:1, 1:2])
                e.reg_alu(dif, f0, want, ALU.not_equal)
                e.reg_alu(d1, f1, want, ALU.not_equal)
                e.reg_alu(dif, dif, d1, ALU.bitwise_or)
            e.free_register(f0); e.free_register(f1); e.free_register(want)
            e.free_register(dif); e.free_register(d1)
            return None
        return self.op("pool", f, [(self.shf, flag_idx), self._nonce_sb],
                       [(self.shf, ("polled", flag_idx)), cell], tag=tag)


def ts(i, n):
    return slice(i * n, (i + 1) * n)


T = 2048
NT = 16
D = 1024
DC = 8
LNEPS = 1e-5


class Scratch:

    def __init__(self, P):
        self.bn = [P.sbuf(f"lnbn{i}", [128, 12], F32) for i in range(2)]
        self.mv = [P.sbuf(f"lnmv{i}", [128, 2], F32) for i in range(2)]
        self.rs = [P.sbuf(f"lnrs{i}", [128, 1], F32) for i in range(2)]
        self.stage = [P.sbuf(f"lnstage{i}", [128, 1024], BF16) for i in range(2)]
        self.tmp = [P.sbuf(f"lntmp{i}", [128, 1024], F32) for i in range(2)]


def emit_ln_stats(P, sc, par, src_ap, src_reads, C=1024, eps=LNEPS):
    bn, mv, rs = sc.bn[par], sc.mv[par], sc.rs[par]
    nch = (C + 511) // 512

    def f_bn(e):
        inst = None
        for c in range(nch):
            w = min(512, C - 512 * c)
            inst = e.bn_stats(bn.t[:, 6 * c:6 * c + 6], src_ap[:, 512 * c:512 * c + w])
        return inst
    P.dve(f_bn, src_reads, [bn], tag="bn")
    P.dve(lambda e: e.bn_aggr(mv.t[:, :], bn.t[:, 0:6 * nch]), [bn], [mv], tag="aggr")
    P.act(rs.t[:, :], mv.t[:, 1:2], AF.Ln, [mv], [rs], bias=float(eps), tag="lnv")
    P.act(rs.t[:, :], rs.t[:, :], AF.Exp, [rs], [rs], scale=-0.5, tag="rstd")
    return mv, rs


def emit_ln_apply(P, sc, par, src_ap, out_ap, src_reads, out_writes, C=1024):
    mv, rs = sc.mv[par], sc.rs[par]
    P.dve(lambda e: e.tensor_scalar(out_ap, src_ap, mv.t[:, 0:1], rs.t[:, 0:1],
                                    ALU.subtract, ALU.mult),
          list(src_reads) + [mv, rs], out_writes, tag="lnap")


def emit_tr_tile(P, banks, ident, src_bf_ap, src_reads, fm, fm_tile_writer, t,
                 nch=8, fmT=T):
    for g in range((nch + 3) // 4):
        bank = banks[6 + (t + g) % 2]
        cw = min(4, nch - 4 * g)
        pairs = []
        for c in range(cw):
            pairs.append((bank.t[:, 128 * c:128 * c + 128],
                          src_bf_ap[:, 128 * (4 * g + c):128 * (4 * g + c) + 128]))
        P.tr(pairs, ident, src_reads, [bank], tag=f"tr{t}")
        out_ap = fm.t[:, :].rearrange("p (c u) -> p c u", c=nch)[:, 4 * g:4 * g + cw,
                                                                128 * t:128 * t + 128]
        P.dve(lambda e, b=bank, o=out_ap, w=cw: e.tensor_copy(o, b.t[:, 0:128 * w]),
              [bank], [(fm, fm_tile_writer(t, g))], tag=f"trev{t}")


def emit_ln_fm(P, sc, banks, ident, x, fm, eps=LNEPS, xkey=None, retain=None):
    for t in range(NT):
        par = t % 2
        xa = x.t[:, 1024 * t:1024 * t + 1024]
        xr = [(x, xkey(t) if xkey else None)]
        emit_ln_stats(P, sc, par, xa, xr, eps=eps)
        stage = sc.stage[par]
        emit_ln_apply(P, sc, par, xa, stage.t[:, :], xr, [stage])
        if retain is not None:
            P.dve(lambda e, t=t, s=stage: e.tensor_copy(
                retain.t[:, 1024 * t:1024 * t + 1024], s.t[:, :]),
                [stage], [(retain, t)], tag="lnkeep")
        emit_tr_tile(P, banks, ident, stage.t[:, :], [stage], fm,
                     lambda tt, g: ("t", tt, g), t)


def load_weight_fm(P, name, dram_ap2d, dIn, F, eng="pool", side="hi"):
    nch = dIn // 128
    w = P.sbuf(name, [128, nch * F], BF16, side=side)
    src = dram_ap2d.rearrange("(c p) f -> p c f", p=128)
    dst = w.t[:, :].rearrange("p (c f) -> p c f", f=F)
    P.dma("pool", dst, src, [], [w], tag=f"ldw_{name}")
    return w


def load_plain(P, name, dram_ap, shape, dtype, eng="sp", side="hi"):
    b = P.sbuf(name, shape, dtype, side=side)
    P.dma(eng, b.t[:, :] if shape[0] > 1 else b.t[0:shape[0], :], dram_ap, [], [b],
          tag=f"ld_{name}")
    return b


def emit_linear_fm(P, banks, rhs_fm, w_sb, out_fm, Fout, func=None, biasT=None,
                   bias_col=0, Tn=T, scale=1.0, rhs_key=None, out_key=None,
                   w_reads=None, extra_mm=None, tag="lin"):
    nf = Fout // 128
    ng = Tn // 512
    w_reads = w_reads or [w_sb]
    for fc in range(nf):
        for g in range(ng):
            bank = banks[(fc * ng + g) % 6]
            groups = []
            for dc in range(DC):
                groups.append((bank.t[:, :],
                               w_sb.t[:, (dc * Fout + 128 * fc):(dc * Fout + 128 * fc) + 128],
                               rhs_fm.t[:, dc * Tn + 512 * g: dc * Tn + 512 * g + 512],
                               dc == 0, False, None))
            if extra_mm is not None:
                groups += extra_mm(fc, g, bank)
            groups[-1] = groups[-1][:4] + (True,) + groups[-1][5:]
            rr = [(rhs_fm, rhs_key(g) if rhs_key else None)] + w_reads
            P.mm(groups, rr, [bank], tag=f"{tag}_mm")
            oap = out_fm.t[:, fc * Tn + 512 * g: fc * Tn + 512 * g + 512]
            wr = [(out_fm, out_key(fc, g) if out_key else None)]
            if func is None and biasT is None and scale == 1.0:
                P.dve(lambda e, o=oap, b=bank: e.tensor_copy(o, b.t[:, :]),
                      [bank], wr, tag=f"{tag}_ev")
            elif func in (None, AF.Copy):
                bias = biasT.t[:, bias_col + fc:bias_col + fc + 1] if biasT is not None else 0.0
                br = [biasT] if biasT is not None else []
                P.dve(lambda e, o=oap, b=bank, bi=bias, s=scale: e.tensor_scalar(
                    o, b.t[:, :], bi, s, ALU.add, ALU.mult),
                    [bank] + br, wr, tag=f"{tag}_ev")
            else:
                bias = biasT.t[:, bias_col + fc:bias_col + fc + 1] if biasT is not None else 0.0
                br = [biasT] if biasT is not None else []
                P.act(oap, bank.t[:, :], func, [bank] + br, wr,
                      bias=bias, scale=scale, tag=f"{tag}_ev")


def emit_linear_tm(P, banks, lhs_fm, w_sb, Fout, evac, Tn=T, lhs_key=None,
                   nch=DC, lhsT_cols=None, extra_mm=None, w_reads=None, tag="ltm"):
    lw = lhsT_cols or Tn
    nh = Fout // 512
    w_reads = w_reads or [w_sb]
    for t in range(Tn // 128):
        for h in range(nh):
            bank = banks[(t * nh + h) % 6]
            groups = []
            for dc in range(nch):
                groups.append((bank.t[:, :],
                               lhs_fm.t[:, dc * lw + 128 * t: dc * lw + 128 * t + 128],
                               w_sb.t[:, dc * Fout + 512 * h: dc * Fout + 512 * h + 512],
                               dc == 0, False, None))
            if extra_mm is not None:
                groups += extra_mm(t, h, bank)
            groups[-1] = groups[-1][:4] + (True,) + groups[-1][5:]
            rr = [(lhs_fm, lhs_key(t) if lhs_key else None)] + w_reads
            P.mm(groups, rr, [bank], tag=f"{tag}_mm")
            evac(t, h, bank)


def emit_moe(P, sc, banks, ident, x, io, xkey=None, dump=(), nc=None):
    E, H = 15, 1024
    rW = load_weight_fm(P, "rW", io["rW"].t.ap(), D, 16)
    rb_bc = load_plain(P, "rb_bc", io["rb_bc"].t.ap()[:, :], [128, 16], F32)
    sW1 = load_weight_fm(P, "sW1", io["sW1"].t.ap(), D, H)
    sW2 = load_weight_fm(P, "sW2", io["sW2"].t.ap(), D, H)
    sb1T = load_plain(P, "sb1T", io["sb1T"].t.ap()[:, :], [128, 8], F32)
    sb2_bc = load_plain(P, "sb2_bc", io["sb2_bc"].t.ap()[:, :], [128, 1024], F32)
    eb1T = load_plain(P, "eb1T", io["eb1T"].t.ap()[:, :], [128, 15 * 8], F32)
    eb2 = P.sbuf("eb2", [15, 1024], BF16, side="hi")
    P.dma("pool", eb2.t[0:15, :], io["eb2"].t.ap()[:, :], [], [eb2], tag="ld_eb2")

    ln_fm = P.sbuf("ln3fm", [128, DC * T], BF16)
    emit_ln_fm(P, sc, banks, ident, x, ln_fm, xkey=xkey)
    if "ln_fm" in dump:
        _dump(P, nc, ln_fm, "ln_fm")

    xspill = P.dram("xspill3", [128, NT * 1024], F32)
    P.dma("sp", xspill.t.ap()[:, :], x.t[:, :], [x], [xspill], tag="xspill")
    P.free(x)
    x = None
    racc = P.sbuf("racc", [128, NT * 1024], F32)

    cw = P.sbuf("cw", [128, NT * 16], F32)
    cwT = P.sbuf("cwT", [16, T], BF16)
    lg = P.sbuf("lg", [128, 16], F32)
    pr = P.sbuf("pr", [128, 16], F32)
    pm = P.sbuf("pm", [128, 16], F32)
    m1 = P.sbuf("m1", [128, 1], F32)
    ssum = P.sbuf("ssum", [128, 1], F32)
    for t in range(NT):
        bank = banks[t % 6]
        groups = []
        for dc in range(DC):
            groups.append((bank.t[:, 0:16],
                           ln_fm.t[:, dc * T + 128 * t: dc * T + 128 * t + 128],
                           rW.t[:, dc * 16:dc * 16 + 16],
                           dc == 0, dc == DC - 1, None))
        P.mm(groups, [(ln_fm, ("t", t)), rW], [bank], tag="router")
        P.dve(lambda e, b=bank: e.tensor_add(lg.t[:, :], b.t[:, 0:16], rb_bc.t[:, :]),
              [bank, rb_bc], [lg], tag="rlg")
        P.dve(lambda e: e.reduce_max(m1.t[:, :], lg.t[:, 0:15], axis=AX, negate=True),
              [lg], [m1], tag="rmax")
        P.act(pr.t[:, :], lg.t[:, :], AF.Exp, [lg, m1], [pr, ssum], bias=m1.t[:, 0:1],
              accum_out=ssum.t[:, 0:1], tag="rexp")
        P.dve(lambda e: e.reciprocal(m1.t[:, :], ssum.t[:, :]), [ssum], [m1], tag="rrec")
        P.dve(lambda e: e.tensor_scalar(pr.t[:, :], pr.t[:, :], m1.t[:, 0:1], None,
                                        ALU.mult), [pr, m1], [pr], tag="rnorm")
        P.dve(lambda e: e.reduce_max(m1.t[:, :], pr.t[:, 0:15], axis=AX), [pr], [m1], tag="t1")
        P.dve(lambda e: e.tensor_scalar(pm.t[:, :], pr.t[:, :], m1.t[:, 0:1], None,
                                        ALU.is_lt), [pr, m1], [pm], tag="t1m")
        P.dve(lambda e: e.tensor_mul(pm.t[:, :], pm.t[:, :], pr.t[:, :]), [pm, pr], [pm], tag="t1z")
        P.dve(lambda e: e.reduce_max(m1.t[:, :], pm.t[:, 0:15], axis=AX), [pm], [m1], tag="t2")
        P.dve(lambda e: e.tensor_scalar(lg.t[:, :], pm.t[:, :], m1.t[:, 0:1], None,
                                        ALU.is_lt), [pm, m1], [lg], tag="t2m")
        P.dve(lambda e: e.tensor_mul(pm.t[:, :], pm.t[:, :], lg.t[:, :]), [pm, lg], [pm], tag="t2z")
        P.dve(lambda e: e.reduce_max(m1.t[:, :], pm.t[:, 0:15], axis=AX), [pm], [m1], tag="t3")
        P.dve(lambda e: e.tensor_scalar(pm.t[:, :], pr.t[:, :], m1.t[:, 0:1], None,
                                        ALU.is_ge), [pr, m1], [pm], tag="keep")
        P.dve(lambda e, t=t: e.tensor_mul(cw.t[:, 16 * t:16 * t + 16], pm.t[:, :],
                                          pr.t[:, :]), [pm, pr], [(cw, t)], tag="cw")
        stage = sc.stage[t % 2]
        P.dve(lambda e, t=t, s=stage: e.tensor_copy(s.t[:, 0:16], cw.t[:, 16 * t:16 * t + 16]),
              [(cw, t)], [(stage, "cwb")], tag="cwb")
        bank6 = banks[6 + t % 2]
        P.tr([(bank6.t[0:16, 0:128], stage.t[:, 0:16])], ident,
             [(stage, "cwb")], [bank6], tag="cwtr")
        P.dve(lambda e, t=t, b6=bank6: e.tensor_copy(
            cwT.t[0:16, 128 * t:128 * t + 128], b6.t[0:16, 0:128]), [bank6],
            [(cwT, t)], tag="cwTev")

    if "cw" in dump:
        _dump(P, nc, cw, "cw")
        _dump(P, nc, cwT, "cwT", rows=16)
    hs_fm = P.sbuf("hs_fm", [128, DC * T], BF16, side="hi")
    emit_linear_fm(P, banks, ln_fm, sW1, hs_fm, H, func=AF.Relu, biasT=sb1T,
                   rhs_key=lambda g: None, out_key=lambda fc, g: ("h", fc, g), tag="sh1")
    P.free(sW1)
    P.free(rW)

    def evac_base(t, h, bank):
        oap = racc.t[:, 1024 * t + 512 * h: 1024 * t + 512 * h + 512]
        P.dve(lambda e, o=oap, b=bank, h=h: e.tensor_add(
            o, b.t[:, :], sb2_bc.t[:, 512 * h:512 * h + 512]),
            [bank, sb2_bc], [(racc, (t, h))], tag="base_ev")

    def extra_base(t, h, bank):
        return [(bank.t[:, :], cwT.t[0:15, 128 * t:128 * t + 128],
                 eb2.t[0:15, 512 * h:512 * h + 512], False, False, None)]

    emit_linear_tm(P, banks, hs_fm, sW2, H, evac_base, lhs_key=lambda t: None,
                   extra_mm=extra_base, w_reads=[sW2, cwT, eb2], tag="sh2")
    P.free(hs_fm)
    P.free(sW2)

    if "base" in dump:
        _dump(P, nc, racc, "base")
    he_fm = P.sbuf("he_fm", [128, DC * T], BF16, side="hi")
    ew1 = [P.sbuf("ew1_0", [128, DC * H], BF16, side="hi")]
    ew2 = [P.sbuf(f"ew2_{i}", [128, DC * D], BF16, side="hi") for i in range(2)]
    for e_i in range(15):
        w1b, w2b = ew1[0], ew2[e_i % 2]
        src1 = io["eW1"].t.ap()[e_i].rearrange("(c p) f -> p c f", p=128)
        src2 = io["eW2"].t.ap()[e_i].rearrange("(c p) f -> p c f", p=128)
        d1 = w1b.t[:, :].rearrange("p (c f) -> p c f", c=DC)
        d2 = w2b.t[:, :].rearrange("p (c f) -> p c f", c=DC)
        P.dma("pool", d1, src1, [], [w1b], tag=f"ldw1_{e_i}")
        P.dma("pool", d2, src2, [], [w2b], tag=f"ldw2_{e_i}")
        emit_linear_fm(P, banks, ln_fm, w1b, he_fm, H, func=AF.Relu, biasT=eb1T,
                       bias_col=8 * e_i, out_key=lambda fc, g: ("h", fc, g),
                       w_reads=[w1b], tag=f"e{e_i}h")

        def evac_ye(t, h, bank, e_i=e_i):
            oap = racc.t[:, 1024 * t + 512 * h: 1024 * t + 512 * h + 512]
            P.dve(lambda e, o=oap, b=bank, t=t, e_i=e_i: e.scalar_tensor_tensor(
                o, b.t[:, :], cw.t[:, 16 * t + e_i:16 * t + e_i + 1], o,
                ALU.mult, ALU.add), [bank, (cw, t), (racc, (t, h))],
                [(racc, (t, h))], tag="ye_ev")

        emit_linear_tm(P, banks, he_fm, w2b, D, evac_ye,
                       lhs_key=lambda t: None, w_reads=[w2b], tag=f"e{e_i}y")
    P.free(he_fm)
    for b in ew1 + ew2:
        P.free(b)
    P.free(ln_fm)

    if "racc" in dump:
        _dump(P, nc, racc, "racc")
    xstage = P.sbuf("xstage", [128, 2 * 1024], F32, side="hi")
    for t in range(NT):
        par = t % 2
        P.dma("sp", xstage.t[:, 1024 * par:1024 * par + 1024],
              xspill.t.ap()[:, 1024 * t:1024 * t + 1024], [xspill],
              [(xstage, par)], tag="xrld")
        ra = racc.t[:, 1024 * t:1024 * t + 1024]
        rkeys = [(racc, (t, 0)), (racc, (t, 1))]
        emit_ln_stats(P, sc, par, ra, rkeys)
        tmp = sc.tmp[par]
        emit_ln_apply(P, sc, par, ra, tmp.t[:, :], rkeys, [tmp])
        P.dve(lambda e, t=t, m=tmp, par=par: e.tensor_add(
            racc.t[:, 1024 * t:1024 * t + 1024],
            xstage.t[:, 1024 * par:1024 * par + 1024], m.t[:, :]),
            [tmp, (xstage, par)], rkeys, tag="resid")
    P.free(xstage)
    for b in (rb_bc, sb1T, sb2_bc, eb1T, eb2, cw, cwT, lg, pr, pm, m1, ssum):
        P.free(b)
    return racc


def _prog_setup(nc):
    P = Prog(nc)
    banks = P.psum_init()
    sc = Scratch(P)
    identh = nc.inline_tensor(np.eye(128, dtype=BF), "ident")
    ident = P.sbuf("ident", [128, 128], BF16)
    P.dma("sp", ident.t[:, :], identh.ap()[:, :], [], [ident], tag="ld_ident")
    for cv in (LNEPS, -0.5, 1e-24):
        P.constf(cv)
    return P, banks, sc, ident


def _load_stream(P, x, xin_buf):
    src = xin_buf.t.ap().rearrange("(t p) d -> p t d", p=128)
    dst = x.t[:, :].rearrange("p (t d) -> p t d", d=D)
    P.dma("sp", dst, src, [xin_buf], [x], tag="ld_x")


def _store_stream(P, x, xout_buf):
    dst = xout_buf.t.ap().rearrange("(t p) d -> p t d", p=128)
    src = x.t[:, :].rearrange("p (t d) -> p t d", d=D)
    P.dma("pool", dst, src, [x], [xout_buf], tag="st_x")


def _dump(P, nc, buf, name, rows=128):
    h = nc.dram_tensor(f"dbg_{name}", [rows, buf.t.shape[1]], buf.t.dtype,
                       kind="ExternalOutput")
    P.dma("sp", h.ap()[:, :], buf.t[0:rows, :], [buf], [P.wrap(h, name, "dram")],
          tag=f"dump_{name}")


def build_slot3_test(dump=()):
    nc = _new_nc()
    xin = nc.dram_tensor("xin", [T, D], F32, kind="ExternalInput")
    xout = nc.dram_tensor("xout", [T, D], F32, kind="ExternalOutput")
    names = {
        "rW": [D, 16], "rb_bc": [128, 16], "sW1": [D, D], "sb1T": [128, 8],
        "sW2": [D, D], "sb2_bc": [128, D], "eW1": [15, D, D],
        "eb1T": [128, 15 * 8], "eW2": [15, D, D], "eb2": [15, D],
    }
    P, banks, sc, ident = _prog_setup(nc)
    io = {k: P.wrap(nc.dram_tensor(k, s, F32, kind="ExternalInput"), k, "dram")
          for k, s in names.items()}
    x = P.sbuf("x", [128, NT * 1024], F32)
    _load_stream(P, x, P.wrap(xin, "xin", "dram"))
    x = emit_moe(P, sc, banks, ident, x, io, xkey=lambda t: t, dump=dump, nc=nc)
    _store_stream(P, x, P.wrap(xout, "xout", "dram"))
    P.emit()
    return nc


def moe_host_prep(w):
    rW = np.zeros((D, 16), np.float32)
    rW[:, :15] = w["L1_ffn_router_W"]
    rb_bc = np.zeros((128, 16), np.float32)
    rb_bc[:, :15] = np.broadcast_to(w["L1_ffn_router_b"], (128, 15))
    rb_bc[:, 15] = -1e30
    sb1T = np.ascontiguousarray(w["L1_ffn_sb1"].reshape(8, 128).T)
    sb2_bc = np.ascontiguousarray(np.broadcast_to(w["L1_ffn_sb2"], (128, D)))
    eb1T = np.ascontiguousarray(
        w["L1_ffn_eb1"].reshape(15, 8, 128).transpose(2, 0, 1).reshape(128, 15 * 8))
    return {
        "rW": rW, "rb_bc": rb_bc, "sW1": np.ascontiguousarray(w["L1_ffn_sW1"]),
        "sb1T": sb1T, "sW2": np.ascontiguousarray(w["L1_ffn_sW2"]),
        "sb2_bc": sb2_bc, "eW1": np.ascontiguousarray(w["L1_ffn_eW1"]),
        "eb1T": eb1T, "eW2": np.ascontiguousarray(w["L1_ffn_eW2"]),
        "eb2": np.ascontiguousarray(w["L1_ffn_eb2"]),
    }


def emit_attn(P, sc, banks, ident, x, io, *, causal, flag_base, xkey=None, pfx="",
              dump=(), nc=None, drip=None):
    NH = 16
    VW = 65 * NH
    ln_fm = P.sbuf(pfx + "ln_fm", [128, DC * T], BF16, side="hi")
    emit_ln_fm(P, sc, banks, ident, x, ln_fm, xkey=xkey)
    xspill = P.dram(pfx + "xspill", [128, NT * 1024], F32)
    P.dma("sp", xspill.t.ap()[:, :], x.t[:, :], [x], [xspill], tag=pfx + "xspill")
    P.free(x)

    v_full = P.sbuf(pfx + "v_full", [128, 32 * VW], BF16)
    k_own = P.sbuf(pfx + "k_own", [128, DC * T], BF16, side="hi")
    wk = load_weight_fm(P, pfx + "Wk", io["Wk"].t.ap(), D, D)
    bkT = load_plain(P, pfx + "bkT", io["bkT"].t.ap()[:, :], [128, 8], F32)
    emit_linear_fm(P, banks, ln_fm, wk, k_own, D, func=AF.Copy, biasT=bkT,
                   out_key=lambda fc, g: ("k", fc, g), tag=pfx + "k")
    P.free(wk)
    P.free(bkT)

    wv = load_weight_fm(P, pfx + "Wv", io["Wv"].t.ap(), D, D)
    bv_bc = load_plain(P, pfx + "bv_bc", io["bv_bc"].t.ap()[:, :], [128, 1024], F32)
    P.pool(lambda e: e.memset(
        v_full.t[:, :].rearrange("p (a b) -> p a b", b=65)[:, :, 64:65], 1.0),
        (), [(v_full, "ones")], tag=pfx + "vones")

    def evac_v(t, h, bank):
        oap = v_full.t[:, :].rearrange("p (a b) -> p a b", b=65)[
            :, 16 * t + 8 * h:16 * t + 8 * h + 8, 0:64]
        P.dve(lambda e, o=oap, b=bank, h=h: e.tensor_add(
            o, b.t[:, :].rearrange("p (a b) -> p a b", b=64),
            bv_bc.t[:, 512 * h:512 * h + 512].rearrange("p (a b) -> p a b", b=64)),
            [bank, bv_bc], [(v_full, ("v", t, h))], tag=pfx + "v_ev")

    emit_linear_tm(P, banks, ln_fm, wv, D, evac_v, tag=pfx + "v")
    P.free(wv)
    P.free(bv_bc)

    shk = P.dram(pfx + "shk", [2, 128, DC * T], BF16, shared=True)
    shv = P.dram(pfx + "shv", [2, 128, 16 * VW], BF16, shared=True)
    P.pdma(lambda par: shk.t.ap()[par], lambda par: k_own.t[:, :],
           [k_own], [(shk, "w")], tag=pfx + "sendk")
    P.pdma(lambda par: shv.t.ap()[par], lambda par: v_full.t[:, 0:16 * VW],
           [(v_full, None)], [(shv, "w")], tag=pfx + "sendv")
    P.send_flag(flag_base, [(shk, "w"), (shv, "w")], tag=pfx + "flag")

    q_fm = P.sbuf(pfx + "q_fm", [128, DC * T], BF16, side="hi")
    wq = load_weight_fm(P, pfx + "Wq", io["Wq"].t.ap(), D, D)
    bqT = load_plain(P, pfx + "bqT", io["bqT"].t.ap()[:, :], [128, 8], F32)
    emit_linear_fm(P, banks, ln_fm, wq, q_fm, D, func=AF.Copy, biasT=bqT,
                   scale=0.125, out_key=lambda fc, g: ("q", fc, g), tag=pfx + "q")
    P.free(wq)
    P.free(bqT)
    P.free(ln_fm)

    if "q_fm" in dump:
        _dump(P, nc, q_fm, "q_fm")
    if "k_own" in dump:
        _dump(P, nc, k_own, "k_own")
    P.poll_flag(flag_base, tag=pfx + "poll")
    P.pdma(lambda par: v_full.t[:, 16 * VW:32 * VW],
           lambda par: shv.t.ap()[1 - par], [(P.shf, ("polled", flag_base))],
           [(v_full, "partner")], tag=pfx + "recvv")

    ctx = P.sbuf(pfx + "ctx", [128, NT * 1024], BF16)
    masks = {}
    if causal:
        masks[0] = load_plain(P, pfx + "m_own", io["mask_own"].t.ap()[:, :],
                              [128, 128], BF16)
        masks[1] = load_plain(P, pfx + "m_par", io["mask_partner"].t.ap()[:, :],
                              [128, 128], BF16)
    k_ring = [P.sbuf(pfx + f"kr{i}", [128, 2 * T], BF16, side="hi") for i in range(2)]
    pT = [[P.sbuf(pfx + f"pT{i}{j}", [128, 512], BF16, side="hi") for j in range(2)]
          for i in range(2)]
    pvu = P.sbuf(pfx + "pvu", [65, 1024], BF16, side="hi")
    rc = P.sbuf(pfx + "rc", [128, 2], F32, side="hi")

    for hp in range(8):
        kr = k_ring[hp % 2]
        P.pdma(lambda par, kr=kr: kr.t[:, 0:T],
               lambda par, hp=hp: shk.t.ap()[par][:, T * hp:T * hp + T],
               [(shk, "w")], [(kr, "own")], tag=pfx + "ldko")
        P.pdma(lambda par, kr=kr: kr.t[:, T:2 * T],
               lambda par, hp=hp: shk.t.ap()[1 - par][:, T * hp:T * hp + T],
               [(P.shf, ("polled", flag_base))], [(kr, "par")], tag=pfx + "ldkp")
        for qg in range(4):
            for _ in range(4):
                if drip:
                    drip.pop(0)()
            nkt = 4 * qg + 4 if causal else 16
            pA, pB = banks[4], banks[5]
            steps = [(src, kt) for src in (0, 1) for kt in range(nkt)]
            pending = None

            def flush(step_i, src, kt):
                N = 512 - max(0, 128 * (kt - 4 * qg)) if causal else 512
                qoff = 512 - N
                bA = banks[(step_i % 2) * 2]
                bB = banks[(step_i % 2) * 2 + 1]
                for hb, (bank, pv_bank) in enumerate(((bA, pA), (bB, pB))):
                    pt = pT[hb][step_i % 2]
                    P.act(pt.t[:, qoff:512], bank.t[:, qoff:512], AF.Exp,
                          [bank], [(pt, "e")], tag=pfx + "exp")
                    if causal and kt >= 4 * qg:
                        m = masks[src]
                        P.dve(lambda e, pt=pt, m=m, qoff=qoff: e.tensor_mul(
                            pt.t[:, qoff:qoff + 128], pt.t[:, qoff:qoff + 128],
                            m.t[:, :]), [(pt, "e"), m], [(pt, "e")], tag=pfx + "msk")
                    vcol = VW * (16 * src + kt) + 65 * (2 * hp + hb)
                    P.mm([(pv_bank.t[0:65, qoff:512],
                           v_full.t[:, vcol:vcol + 65],
                           pt.t[:, qoff:512],
                           step_i == 0, step_i == len(steps) - 1, None)],
                         [(pt, "e"), (v_full, None)], [pv_bank], tag=pfx + "pv")

            for step_i, (src, kt) in enumerate(steps):
                N = 512 - max(0, 128 * (kt - 4 * qg)) if causal else 512
                qoff = 512 - N
                qcol = 2048 * hp + 512 * qg + qoff
                kcol = T * src + 128 * kt
                bA = banks[(step_i % 2) * 2]
                bB = banks[(step_i % 2) * 2 + 1]
                P.mm([(bA.t[:, qoff:512], kr.t[0:64, kcol:kcol + 128],
                       q_fm.t[0:64, qcol:qcol + N], True, True, None),
                      (bB.t[:, qoff:512], kr.t[64:128, kcol:kcol + 128],
                       q_fm.t[64:128, qcol:qcol + N], True, True, (64, 0))],
                     [(kr, "own" if src == 0 else "par"), (q_fm, None)],
                     [bA, bB], tag=pfx + "sc")
                if pending is not None:
                    flush(*pending)
                pending = (step_i, src, kt)
            flush(*pending)
            P.act(pvu.t[0:65, 0:512], pA.t[0:65, :], AF.Copy, [pA],
                  [(pvu, 0)], tag=pfx + "pvev")
            P.act(pvu.t[0:65, 512:1024], pB.t[0:65, :], AF.Copy, [pB],
                  [(pvu, 1)], tag=pfx + "pvev")
            for hb in range(2):
                b67 = banks[6 + hb]
                prs = []
                for s in range(4):
                    prs.append((b67.t[:, 66 * s:66 * s + 65],
                                pvu.t[0:65, 512 * hb + 128 * s:512 * hb + 128 * s + 128]))
                P.tr(prs, ident, [(pvu, hb)], [b67], tag=pfx + "pvtr", ksize=65)
                for s in range(4):
                    t_tile = 4 * qg + s
                    h_global = 2 * hp + hb
                    P.dve(lambda e, b=b67, s=s, hb=hb: e.reciprocal(
                        rc.t[:, hb:hb + 1], b.t[:, 66 * s + 64:66 * s + 65]),
                        [b67], [(rc, hb)], tag=pfx + "rcp")
                    oap = ctx.t[:, 1024 * t_tile + 64 * h_global:
                                1024 * t_tile + 64 * h_global + 64]
                    P.dve(lambda e, b=b67, s=s, hb=hb, o=oap: e.tensor_scalar(
                        o, b.t[:, 66 * s:66 * s + 64], rc.t[:, hb:hb + 1], None,
                        ALU.mult), [b67, (rc, hb)],
                        [(ctx, (t_tile, h_global))], tag=pfx + "ctxev")
    if "ctx" in dump:
        _dump(P, nc, ctx, "ctx")
    if "v_full" in dump:
        _dump(P, nc, v_full, "v_full")
    P.free(q_fm)
    for b in k_ring:
        P.free(b)
    P.free(v_full)
    P.free(k_own)
    P.free(pvu)
    P.free(rc)
    for b in pT[0] + pT[1]:
        P.free(b)
    for m in masks.values():
        P.free(m)

    x = P.sbuf("x", [128, NT * 1024], F32)
    for t in range(NT):
        par = t % 2
        P.dma("sp", x.t[:, 1024 * t:1024 * t + 1024],
              xspill.t.ap()[:, 1024 * t:1024 * t + 1024], [xspill],
              [(x, xkey(t) if xkey else None)], tag=pfx + "xrld")
        ckeys = [(ctx, (t, h)) for h in range(16)]
        ca = ctx.t[:, 1024 * t:1024 * t + 1024]
        emit_ln_stats(P, sc, par, ca, ckeys)
        tmp = sc.tmp[par]
        emit_ln_apply(P, sc, par, ca, tmp.t[:, :], ckeys, [tmp])
        P.dve(lambda e, t=t, m=tmp: e.tensor_add(
            x.t[:, 1024 * t:1024 * t + 1024], x.t[:, 1024 * t:1024 * t + 1024],
            m.t[:, :]), [tmp, (x, xkey(t) if xkey else None)],
            [(x, xkey(t) if xkey else None)], tag=pfx + "resid")
    P.free(ctx)
    return x


def build_slot_attn_test(causal=True, dump=()):
    nc = _new_nc()
    xin = nc.dram_tensor("xin", [T, D], F32, kind="ExternalInput")
    xout = nc.dram_tensor("xout", [T, D], F32, kind="ExternalOutput")
    P, banks, sc, ident = _prog_setup(nc)
    parity = P.wrap(nc.dram_tensor("parity", [1, 1], U32, kind="ExternalInput"),
                    "parity", "dram")
    nonce = P.wrap(nc.dram_tensor("nonce", [1, 4], U32, kind="ExternalInput"),
                   "nonce", "dram")
    P.setup_parity(parity, nonce, 4)
    io = {}
    for k, s in [("Wq", [D, D]), ("Wk", [D, D]), ("Wv", [D, D]),
                 ("bqT", [128, 8]), ("bkT", [128, 8]), ("bv_bc", [128, 1024])]:
        io[k] = P.wrap(nc.dram_tensor(k, s, F32, kind="ExternalInput"), k, "dram")
    if causal:
        for k in ("mask_own", "mask_partner"):
            io[k] = P.wrap(nc.dram_tensor(k, [128, 128], BF16, kind="ExternalInput"),
                           k, "dram")
    x = P.sbuf("x", [128, NT * 1024], F32)
    _load_stream(P, x, P.wrap(xin, "xin", "dram"))
    x = emit_attn(P, sc, banks, ident, x, io, causal=causal, flag_base=0,
                  xkey=lambda t: t, pfx="a_", dump=dump, nc=nc)
    _store_stream(P, x, P.wrap(xout, "xout", "dram"))
    P.emit()
    return nc


def attn_host_prep(Wq, bq, Wk, bk, Wv, bv, parity, causal, pre_transposed):
    def wl(W):
        W = np.ascontiguousarray(W.T) if pre_transposed else np.ascontiguousarray(W)
        return W
    def bT(b):
        return np.ascontiguousarray(b.reshape(8, 128).T)
    out = {
        "Wq": wl(Wq), "Wk": wl(Wk), "Wv": wl(Wv),
        "bqT": bT(bq), "bkT": bT(bk),
        "bv_bc": np.ascontiguousarray(np.broadcast_to(bv, (128, 1024))),
    }
    if causal:
        j = np.arange(128)[:, None]
        i = np.arange(128)[None, :]
        out["mask_own"] = (j <= i).astype(BF)
        out["mask_partner"] = (j <= i - 1 + parity).astype(BF)
    return out


INNER = 2816
NIC = 22


def emit_ffn_casts(P, io, pfx="f_"):
    w1g_bf = P.dram(pfx + "w1g_bf", [2, D, INNER], BF16)
    w2_bf = P.dram(pfx + "w2_bf", [2, INNER, D], BF16)
    cs = []

    def _cs():
        if not cs:
            cs.extend(P.sbuf(pfx + f"cs{i}", [128, 1408], BF16, side="hi")
                      for i in range(2))
        return cs
    pieces = []
    i = 0
    for l in range(2):
        jobs = [(io["w1g"], w1g_bf, INNER), (io["w2"], w2_bf, D)]
        for (srcb, dstb, wd) in jobs:
            nrc = 8 * INNER // wd
            nh = 2 if wd == INNER else 1
            for c in range(nrc):
                for hf in range(nh):
                    cw = wd // nh
                    def piece(l=l, srcb=srcb, dstb=dstb, c=c, i=i,
                              c0=hf * (wd // nh), cw=cw):
                        b = _cs()[i % 2]
                        P.dma("pool", b.t[:, 0:cw],
                              srcb.t.ap()[l][128 * c:128 * c + 128,
                                             c0:c0 + cw],
                              [srcb], [b], tag=pfx + "cld")
                        P.dma("pool",
                              dstb.t.ap()[l][128 * c:128 * c + 128, c0:c0 + cw],
                              b.t[:, 0:cw], [b], [(dstb, l)], tag=pfx + "cst")
                    pieces.append(piece)
                    i += 1

    def fin():
        for b in cs:
            P.free(b)
    return (w1g_bf, w2_bf), pieces, fin


def emit_l0ffn(P, sc, banks, ident, x, io, *, flag_base, xkey=None, pfx="f_",
               dump=(), nc=None, stop_after="", precast=None):
    if precast is None:
        precast, _pieces, _fin = emit_ffn_casts(P, io, pfx)
        for _pc in _pieces:
            _pc()
        _fin()
    w1g_bf, w2_bf = precast

    xspill = P.dram(pfx + "xspill", [128, NT * 1024], F32)
    P.dma("sp", xspill.t.ap()[:, :], x.t[:, :], [x], [xspill], tag=pfx + "xspill")
    xint = x
    for t in range(NT):
        par = t % 2
        xa = xint.t[:, 1024 * t:1024 * t + 1024]
        xr = [(xint, xkey(t) if xkey else None), xspill]
        emit_ln_stats(P, sc, par, xa, xr)
        emit_ln_apply(P, sc, par, xa, xa, xr, [(xint, ("x", t))])

    tabs = {k: load_plain(P, pfx + k, io[k].t.ap()[:, :], [128, 16 * 64], BF16,
                          eng="pool")
            for k in ("Aq", "Bq", "Ak", "Bk")}
    qs_bc = load_plain(P, pfx + "qs", io["qs_bc"].t.ap()[:, :], [128, 2 * 64], F32)
    ks_bc = load_plain(P, pfx + "ks", io["ks_bc"].t.ap()[:, :], [128, 2 * 64], F32)
    amasks = {k: load_plain(P, pfx + k, io[k].t.ap()[:, :], [128, 128], F32)
              for k in ("am_od", "am_of", "am_pd", "am_pf")}

    b1aT = load_plain(P, pfx + "b1aT", io["b1aT"].t.ap()[:, :], [128, 2 * NIC], F32)
    b1gT = load_plain(P, pfx + "b1gT", io["b1gT"].t.ap()[:, :], [128, 2 * NIC], F32)

    onesr = P.sbuf(pfx + "onesr", [1, 128], BF16, side="hi")
    P.pool(lambda e: e.memset(onesr.t[:, :], 1.0), (), [onesr], tag=pfx + "ones")
    zrow = P.sbuf(pfx + "zrow", [1, 512], BF16, side="hi")
    P.pool(lambda e: e.memset(zrow.t[:, :], 0.0), (), [zrow], tag=pfx + "z0")
    zdram = P.dram(pfx + "zdram", [1, 512], BF16)
    P.dma("pool", zdram.t.ap()[:, :], zrow.t[:, :], [zrow], [zdram], tag=pfx + "zd")
    qk_st = [P.sbuf(pfx + f"qkst{i}", [128, 512], F32, side="hi") for i in range(1)]
    l2s = [P.sbuf(pfx + f"l2s{i}", [128, 16], F32, side="hi") for i in range(2)]

    def transpose_tm_to_fm(src_tm_ap_fn, src_reads_fn, dst_fm, ncols, name,
                           chunk_cols, src_bf=False, tiles=None, t0=0):
        nch = ncols // 128
        for t in (tiles if tiles is not None else range(NT)):
            par = t % 2
            if src_bf:
                stage_ap = src_tm_ap_fn(t)
                sreads = src_reads_fn(t)
            else:
                stage = sc.stage[par]
                P.dve(lambda e, o=stage, a=src_tm_ap_fn(t), w=ncols:
                      e.tensor_copy(o.t[:, 0:w], a), src_reads_fn(t),
                      [(stage, name)], tag=pfx + "cst")
                stage_ap = stage.t[:, 0:ncols]
                sreads = [(stage, name)]
            for g in range((nch + 3) // 4):
                bank = banks[6 + (t + g) % 2]
                cw = min(4, nch - 4 * g)
                prs = []
                for c in range(cw):
                    prs.append((bank.t[:, 128 * c:128 * c + 128],
                                stage_ap[:, 128 * (4 * g + c):128 * (4 * g + c) + 128]))
                P.tr(prs, ident, sreads, [bank], tag=pfx + "tr" + name)
                oap = dst_fm.t[:, :].rearrange(
                    "p (c u) -> p c u", u=chunk_cols)[
                    :, 4 * g:4 * g + cw, 128 * (t - t0):128 * (t - t0) + 128]
                P.dve(lambda e, b=bank, o=oap, w=cw: e.tensor_copy(
                    o, b.t[:, 0:128 * w]), [bank], [(dst_fm, (name, t, g))],
                    tag=pfx + "trev" + name)

    def shift_recv(dst, sh, flagkey, tiles0, ntl, tag, dst_col0=0):
        def outs(par):
            v = dst.t[:, dst_col0:dst_col0 + 512 * ntl].rearrange(
                "p (t f) -> p t f", f=512)
            return [v[1:128, :, :], v[0:1, 1:ntl, :],
                    dst.t[0:1, dst_col0:dst_col0 + 512]]

        def ins(par):
            r = sh.t.ap()[1 - par].rearrange("(t p) f -> p t f", p=128)
            if par == 1:
                return [r[1:128, tiles0:tiles0 + ntl, :],
                        r[0:1, tiles0 + 1:tiles0 + ntl, :],
                        sh.t.ap()[1 - par][128 * tiles0:128 * tiles0 + 1, :]]
            first = (zdram.t.ap()[0:1, :] if tiles0 == 0 else
                     sh.t.ap()[1 - par][128 * tiles0 - 1:128 * tiles0, :])
            return [r[0:127, tiles0:tiles0 + ntl, :],
                    r[127:128, tiles0:tiles0 + ntl - 1, :], first]
        P.pdma(outs, ins, [(P.shf, flagkey), zdram], [(dst, "d")], tag=tag, n=3)

    class _Stop(Exception):
        pass

    def _chk(name):
        if stop_after == name:
            raise _Stop()

    def send_half_bf(sh, flag_idx, tag):
        xh_bf = P.sbuf(pfx + tag + "_xh", [128, NT * 512], BF16, side="hi")
        for hh in range(2):
            t0, t1 = 8 * hh, 8 * hh + 8
            P.dve(lambda e, xh_bf=xh_bf, t0=t0: e.tensor_copy(
                xh_bf.t[:, 4096 * (t0 // 8):4096 * (t0 // 8) + 4096].rearrange(
                    "p (t f) -> p t f", f=512),
                xint.t[:, 1024 * t0:1024 * t0 + 8192].rearrange(
                    "p (t f) -> p t f", f=1024)[:, :, 512:1024]),
                [(xint, ("x", t)) for t in range(t0, t1)], [(xh_bf, hh)],
                tag=tag + "c")
            P.pdma(lambda par, sh=sh, t0=t0: sh.t.ap()[par][
                       128 * t0:128 * t0 + 1024, :].rearrange(
                       "(t p) f -> p t f", p=128),
                   lambda par, xh_bf=xh_bf, t0=t0: xh_bf.t[
                       :, 4096 * (t0 // 8):4096 * (t0 // 8) + 4096].rearrange(
                       "p (t f) -> p t f", f=512),
                   [(xh_bf, hh)], [(sh, ("w", hh))], tag=tag)
        P.send_flag(flag_idx, [(sh, ("w", 0)), (sh, ("w", 1))], tag=tag + "f")
        P.free(xh_bf)

    def _layer(l):
        sh1 = P.dram(pfx + f"sh1_{l}", [2, T, 512], BF16, shared=True)
        send_half_bf(sh1, flag_base + 3 * l, pfx + "send1")
        P.poll_flag(flag_base + 3 * l, tag=pfx + "poll1")
        shif_tm = P.sbuf(pfx + "shif_tm", [128, 4 * 512], BF16, side="hi")

        xint_fm4 = P.sbuf(pfx + "xfm4", [128, 4 * (T // 4)], BF16, side="hi")
        shif_fm = P.sbuf(pfx + "sfm4", [128, 4 * (T // 4)], BF16, side="hi")

        qk_sets = [[P.sbuf(pfx + f"qkc{i}_{j}", [128, 512], F32, side="hi")
                    for j in range(3)] for i in range(2)]
        qk_bf = [P.sbuf(pfx + f"qkbf{i}", [128, 512], BF16, side="hi")
                 for i in range(4)]
        v_own = P.sbuf(pfx + "v_own", [128, 16 * 512], BF16)
        q_fm = P.sbuf(pfx + "q_fm", [128, 4 * T], BF16)
        k_own = P.sbuf(pfx + "k_own", [128, 4 * T], BF16)
        wqkv_sb = load_weight_fm(P, pfx + f"wqkv{l}", io["wqkv"].t.ap()[l], D, 1536)

        Aqe = P.sbuf(pfx + "Aqe", [128, 16 * 64], BF16, side="hi")
        Bqe = P.sbuf(pfx + "Bqe", [128, 16 * 64], BF16, side="hi")
        Ake = P.sbuf(pfx + "Ake", [128, 16 * 64], BF16, side="hi")
        Bke = P.sbuf(pfx + "Bke", [128, 16 * 64], BF16, side="hi")
        for (dst, tab, scb, swap) in ((Aqe, "Aq", qs_bc, False), (Bqe, "Bq", qs_bc, True),
                                      (Ake, "Ak", ks_bc, False), (Bke, "Bk", ks_bc, True)):
            if swap:
                sv = scb.t[:, 64 * l:64 * l + 64].rearrange(
                    "p (o s f) -> p o s f", o=1, s=2)[:, :, ::-1, :]                     .broadcast_to((128, 16, 2, 32))
                oi = dst.t[:, :].rearrange("p (t s f) -> p t s f", s=2, f=32)
                ti = tabs[tab].t[:, :].rearrange("p (t s f) -> p t s f", s=2, f=32)
            else:
                sv = scb.t[:, 64 * l:64 * l + 64].rearrange(
                    "p (o f) -> p o f", o=1).broadcast_to((128, 16, 64))
                oi = dst.t[:, :].rearrange("p (t f) -> p t f", f=64)
                ti = tabs[tab].t[:, :].rearrange("p (t f) -> p t f", f=64)
            P.dve(lambda e, oi=oi, ti=ti, sv=sv: e.tensor_mul(oi, ti, sv),
                  [tabs[tab], scb], [dst], tag=pfx + "tabeff")

        def _qkv_mm(t, ht0):
            par3 = (t % 2) * 3
            for fb in range(3):
                bank = banks[par3 + fb]
                groups = []
                for dc in range(DC):
                    lhs = (xint_fm4 if dc < 4 else shif_fm)
                    lc = (dc if dc < 4 else dc - 4) * (T // 4) + 128 * (t - ht0)
                    groups.append((bank.t[:, :], lhs.t[:, lc:lc + 128],
                                   wqkv_sb.t[:, dc * 1536 + 512 * fb:
                                             dc * 1536 + 512 * fb + 512],
                                   dc == 0, dc == DC - 1, None))
                P.mm(groups, [(xint_fm4, None), (shif_fm, None), wqkv_sb],
                     [bank], tag=pfx + "qkv")

        def _qkv_chain(t):
            par3 = (t % 2) * 3
            bq_, bk_, bv_ = banks[par3], banks[par3 + 1], banks[par3 + 2]
            P.act(v_own.t[:, 512 * t:512 * t + 512], bv_.t[:, :], AF.Copy,
                  [bv_], [(v_own, ("own", t))], tag=pfx + "vev")
            for qk_i, (bsrc, Ae, Be) in enumerate(((bq_, Aqe, Bqe),
                                                   (bk_, Ake, Bke))):
                raw, l2q, rot = qk_sets[qk_i]
                stg = qk_bf[2 * (t % 2) + qk_i]
                rs8 = l2s[qk_i]
                P.act(raw.t[:, :], bsrc.t[:, :], AF.Copy,
                      [bsrc], [(raw, ("v", t))], tag=pfx + "qkcp")
                P.act(rot.t[:, :], raw.t[:, :], AF.Square, [(raw, ("v", t))],
                      [(rot, ("sq", t))], tag=pfx + "sq")
                P.dve(lambda e, s=rot, o=rs8: e.tensor_reduce(
                    o.t[:, 0:8].rearrange("p (a b) -> p a b", b=1),
                    s.t[:, :].rearrange("p (h f) -> p h f", f=64), AX, ALU.add),
                    [(rot, ("sq", t))], [(rs8, ("s", t))], tag=pfx + "ssq")
                P.act(rs8.t[:, 8:16], rs8.t[:, 0:8], AF.Ln, [(rs8, ("s", t))],
                      [(rs8, ("l", t))], bias=1e-24, tag=pfx + "l2ln")
                P.act(rs8.t[:, 8:16], rs8.t[:, 8:16], AF.Exp, [(rs8, ("l", t))],
                      [(rs8, ("r", t))], scale=-0.5, tag=pfx + "l2ex")
                rsb = rs8.t[:, 8:16].rearrange("p (h o) -> p h o", o=1) \
                    .broadcast_to((128, 8, 64))
                P.dve(lambda e, s=raw, o=l2q, r=rsb: e.tensor_mul(
                    o.t[:, :].rearrange("p (h f) -> p h f", f=64),
                    s.t[:, :].rearrange("p (h f) -> p h f", f=64), r),
                    [(raw, ("v", t)), (rs8, ("r", t))], [(l2q, ("v", t))],
                    tag=pfx + "l2ap")
                ab = Ae.t[:, 64 * t:64 * t + 64].rearrange(
                    "p (o f) -> p o f", o=1).broadcast_to((128, 8, 64))
                P.dve(lambda e, i=l2q, o=raw, a=ab: e.tensor_mul(
                    o.t[:, :].rearrange("p (h f) -> p h f", f=64),
                    i.t[:, :].rearrange("p (h f) -> p h f", f=64), a),
                    [(l2q, ("v", t)), Ae], [(raw, ("rq", t))], tag=pfx + "rotA")
                swv = l2q.t[:, :].rearrange("p (h s f) -> p h s f", s=2, f=32)[
                    :, :, ::-1, :]
                bb4 = Be.t[:, 64 * t:64 * t + 64].rearrange(
                    "p (o s f) -> p o s f", o=1, s=2).broadcast_to((128, 8, 2, 32))
                P.dve(lambda e, o=rot, swv=swv, b=bb4: e.tensor_mul(
                    o.t[:, :].rearrange("p (h s f) -> p h s f", s=2, f=32), swv, b),
                    [(l2q, ("v", t)), Be], [(rot, ("rot", t))], tag=pfx + "rotB")
                P.dve(lambda e, a=raw, b2=rot, s=stg: e.tensor_add(
                    s.t[:, :], a.t[:, :], b2.t[:, :]),
                    [(raw, ("rq", t)), (rot, ("rot", t))], [(stg, ("d", t))],
                    tag=pfx + "rotS")

        def _qkv_out(t):
            for qk_i, (dst_fm, cw) in enumerate(((q_fm, T), (k_own, T))):
                stg = qk_bf[2 * (t % 2) + qk_i]
                bank67 = banks[6 + qk_i]
                prs = []
                for hp4 in range(4):
                    for sub in range(2):
                        prs.append((bank67.t[64 * sub:64 * sub + 64,
                                             128 * hp4:128 * hp4 + 128],
                                    stg.t[:, 128 * hp4 + 64 * sub:
                                          128 * hp4 + 64 * sub + 64]))
                P.tr(prs, ident, [(stg, ("d", t))], [bank67], tag=pfx + "qktr")
                oap = dst_fm.t[:, :].rearrange("p (c u) -> p c u", u=cw)[
                    :, :, 128 * t:128 * t + 128]
                P.act(oap, bank67.t[:, 0:512].rearrange("p (c u) -> p c u", u=128),
                      AF.Copy, [bank67], [(dst_fm, ("own", t))], tag=pfx + "qkfm")

        for half in range(4):
            ht0 = 4 * half
            shift_recv(shif_tm, sh1, ("polled", flag_base + 3 * l), ht0, 4,
                       pfx + f"recv1{half}")
            transpose_tm_to_fm(
                lambda t: xint.t[:, 1024 * t:1024 * t + 512],
                lambda t: [(xint, ("x", t))], xint_fm4, 512, "xf", T // 4,
                tiles=range(ht0, ht0 + 4), t0=ht0)
            transpose_tm_to_fm(
                lambda t, ht0=ht0: shif_tm.t[:, 512 * (t - ht0):512 * (t - ht0) + 512],
                lambda t: [(shif_tm, "d")], shif_fm, 512, "sf",
                T // 4, src_bf=True, tiles=range(ht0, ht0 + 4), t0=ht0)
            for t in range(ht0, ht0 + 4):
                _qkv_mm(t, ht0)
                if t >= 1:
                    _qkv_chain(t - 1)
                if t >= 2:
                    _qkv_out(t - 2)
        _qkv_chain(NT - 1)
        _qkv_out(NT - 2)
        _qkv_out(NT - 1)
        P.free(wqkv_sb)
        P.free(xint_fm4)
        P.free(shif_fm)
        P.free(shif_tm)
        for s2 in qk_sets:
            for b in s2:
                P.free(b)
        for b in qk_bf + [Aqe, Bqe, Ake, Bke]:
            P.free(b)

        _chk("qkv")
        sh2k = P.dram(pfx + f"sh2k_{l}", [2, 128, 4 * T], BF16, shared=True)
        sh2v = P.dram(pfx + f"sh2v_{l}", [2, 128, 16 * 512], BF16, shared=True)
        P.pdma(lambda par, sh2k=sh2k: sh2k.t.ap()[par],
               lambda par: k_own.t[:, :],
               [(k_own, None)], [(sh2k, "w")], tag=pfx + "sendk")
        P.pdma(lambda par, sh2v=sh2v: sh2v.t.ap()[par],
               lambda par: v_own.t[:, :],
               [(v_own, None)], [(sh2v, "w")], tag=pfx + "sendv")
        P.send_flag(flag_base + 3 * l + 1, [(sh2k, "w"), (sh2v, "w")],
                    tag=pfx + "flag2")
        P.poll_flag(flag_base + 3 * l + 1, tag=pfx + "poll2")
        k_par = P.sbuf(pfx + "k_par", [128, 4 * 1280], BF16, side="hi")
        v_par = P.sbuf(pfx + "v_par", [128, 10 * 512], BF16, side="hi")

        _chk("exch2")
        o_tm = P.sbuf(pfx + "o_tm", [128, NT * 512], BF16)
        pb2 = [P.sbuf(pfx + f"pb{i}", [128, 768], BF16, side="hi")
               for i in range(2)]
        pbT2 = [P.sbuf(pfx + f"pbT{i}", [128, 768], BF16, side="hi")
                for i in range(2)]
        kbase = 0

        def _la_tailA(st):
            (t, h, rot, bO, bR, bP, kts, nk, W, kb) = st
            mvp = sc.mv[rot]
            pb = pb2[rot]
            pbT = pbT2[rot]
            for (bk, dk, fk) in ((bO, "am_od", "am_of"), (bR, "am_pd", "am_pf")):
                P.dve(lambda e, b=bk, W=W, m=amasks[dk]: e.tensor_add(
                    b.t[:, W - 128:W], b.t[:, W - 128:W], m.t[:, :]),
                    [bk, amasks[dk]], [bk], tag=pfx + "lmsk")
                if nk == 3:
                    P.dve(lambda e, b=bk, m=amasks[fk]: e.tensor_add(
                        b.t[:, 0:128], b.t[:, 0:128], m.t[:, :]),
                        [bk, amasks[fk]], [bk], tag=pfx + "lmsk")
            P.act(pb.t[:, 0:W], bO.t[:, 0:W], AF.Exp,
                  [bO], [(pb, "o"), (sc.rs[rot], "acc")],
                  accum_out=sc.rs[rot].t[:, 0:1], tag=pfx + "lexo")
            P.act(pb.t[:, W:2 * W], bR.t[:, 0:W], AF.Exp,
                  [bR], [(pb, "p"), (sc.tmp[rot], "acc")],
                  accum_out=sc.tmp[rot].t[:, 0:1], tag=pfx + "lexp")
            P.dve(lambda e, r=sc.rs[rot], t2=sc.tmp[rot]: e.tensor_add(
                r.t[:, 0:1], r.t[:, 0:1], t2.t[:, 0:1]),
                [(sc.rs[rot], "acc"), (sc.tmp[rot], "acc")],
                [(sc.rs[rot], "s")], tag=pfx + "lsum")
            P.dve(lambda e, r=sc.rs[rot], m=mvp: e.reciprocal(
                m.t[:, 1:2], r.t[:, 0:1]), [(sc.rs[rot], "s")],
                [(mvp, "rc")], tag=pfx + "lrc")

        def _la_tailB(st):
            (t, h, rot, bO, bR, bP, kts, nk, W, kb) = st
            mvp = sc.mv[rot]
            pb = pb2[rot]
            pbT = pbT2[rot]
            b67 = banks[6 + rot]
            prs = []
            for c in range(2 * nk):
                prs.append((b67.t[:, 128 * c:128 * c + 128],
                            pb.t[:, 128 * c:128 * c + 128]))
            P.tr(prs, ident, [(pb, "o"), (pb, "p")], [b67], tag=pfx + "ltr")
            P.act(pbT.t[:, 0:2 * W], b67.t[:, 0:2 * W], AF.Copy,
                  [b67], [(pbT, "e")], tag=pfx + "ltev")
            groups = []
            for ci in range(2 * nk):
                src = 0 if ci < nk else 1
                kt = kts[ci % nk]
                vcol = (512 * kt if src == 0 else 512 * (kt - kb)) + 64 * h
                vb = v_own if src == 0 else v_par
                groups.append((bP.t[:, 0:64],
                               pbT.t[:, 128 * ci:128 * ci + 128],
                               vb.t[:, vcol:vcol + 64],
                               ci == 0, ci == 2 * nk - 1, None))
            P.mm(groups, [(pbT, "e"), (v_own, None), (v_par, "d")], [bP],
                 tag=pfx + "lpv")
            P.dve(lambda e, b=bP, t=t, h=h, m=mvp: e.tensor_scalar(
                o_tm.t[:, 512 * t + 64 * h:512 * t + 64 * h + 64],
                b.t[:, 0:64], m.t[:, 1:2], None, ALU.mult),
                [bP, (mvp, "rc")], [(o_tm, (t, h))], tag=pfx + "loev")

        pendA = pendB = None

        def _la_flush():
            nonlocal pendA, pendB
            if pendA is not None:
                _la_tailA(pendA)
            if pendB is not None:
                _la_tailB(pendB)
            if pendA is not None:
                _la_tailB(pendA)
            pendA = pendB = None
        for t in range(NT):
            if t % 8 == 0:
                _la_flush()
                kbase = max(0, t - 2)
                nload = t + 8 - kbase
                P.pdma(lambda par, kb=kbase, nl=nload: k_par.t[:, :].rearrange(
                    "p (c u) -> p c u", c=4)[:, :, 0:128 * nl],
                    lambda par, sh2k=sh2k, kb=kbase, nl=nload:
                    sh2k.t.ap()[1 - par].rearrange("p (c u) -> p c u", c=4)[
                        :, :, 128 * kb:128 * kb + 128 * nl],
                    [(P.shf, ("polled", flag_base + 3 * l + 1))], [(k_par, "d")],
                    tag=pfx + "recvk")
                P.pdma(lambda par, kb=kbase, nl=nload: v_par.t[:, 0:512 * nl],
                       lambda par, sh2v=sh2v, kb=kbase, nl=nload:
                       sh2v.t.ap()[1 - par][:, 512 * kb:512 * kb + 512 * nl],
                       [(P.shf, ("polled", flag_base + 3 * l + 1))], [(v_par, "d")],
                       tag=pfx + "recvv")
            kts = [kt for kt in (t - 2, t - 1, t) if kt >= 0]
            nk = len(kts)
            W = 128 * nk
            for h in range(8):
                hp, hb = h // 2, h % 2
                rot = (t + h) % 2
                bO = banks[rot * 2]
                bR = banks[rot * 2 + 1]
                bP = banks[4 + rot]
                tp = (64 * hb, 0) if hb else None
                qs_ap = q_fm.t[64 * hb:64 * hb + 64,
                               T * hp + 128 * t:T * hp + 128 * t + 128]
                P.mm([(bO.t[:, 0:W], qs_ap,
                       k_own.t[64 * hb:64 * hb + 64,
                               T * hp + 128 * kts[0]:T * hp + 128 * kts[0] + W],
                       True, True, tp)],
                     [(q_fm, ("own", t)), (k_own, None)], [bO], tag=pfx + "lsco")
                kpc = 128 * (kts[0] - kbase)
                P.mm([(bR.t[:, 0:W], qs_ap,
                       k_par.t[64 * hb:64 * hb + 64,
                               1280 * hp + kpc:1280 * hp + kpc + W],
                       True, True, tp)],
                     [(q_fm, ("own", t)), (k_par, "d")], [bR], tag=pfx + "lscp")
                if pendA is not None:
                    _la_tailA(pendA)
                if pendB is not None:
                    _la_tailB(pendB)
                pendB = pendA
                pendA = (t, h, rot, bO, bR, bP, kts, nk, W, kbase)
        _la_flush()
        P.free(q_fm)
        P.free(k_own)
        P.free(k_par)
        P.free(v_own)
        P.free(v_par)
        for b in pb2 + pbT2:
            P.free(b)

        _chk("attn")
        w1a_sb = P.sbuf(pfx + "w1a_sb", [128, DC * INNER], BF16, side="lo")
        for dc in range(DC):
            P.dma("pool", w1a_sb.t[:, INNER * dc:INNER * dc + INNER],
                  io["w1a"].t.ap()[l][128 * dc:128 * dc + 128, :],
                  [io["w1a"]], [(w1a_sb, dc)], tag=pfx + "ldw1a")
        o_fm = P.sbuf(pfx + "o_fm", [128, 4 * T], BF16, side="hi")
        transpose_tm_to_fm(lambda t: o_tm.t[:, 512 * t:512 * t + 512],
                           lambda t: [(o_tm, (t, h)) for h in range(8)],
                           o_fm, 512, "of", T, src_bf=True)
        wo_sb = P.sbuf(pfx + "wo_sb", [128, 4 * 1024], BF16, side="hi")
        P.dma("pool", wo_sb.t[:, :].rearrange("p (c f) -> p c f", c=4),
              io["wo"].t.ap()[l].rearrange("(c p) f -> p c f", p=128),
              [io["wo"]], [wo_sb], tag=pfx + "ldwo")
        for t in range(NT):
            for hf in range(2):
                bank = banks[(2 * t + hf) % 4]
                groups = []
                for ic in range(4):
                    groups.append((bank.t[:, :],
                                   o_fm.t[:, ic * T + 128 * t:ic * T + 128 * t + 128],
                                   wo_sb.t[:, ic * 1024 + 512 * hf:
                                           ic * 1024 + 512 * hf + 512],
                                   ic == 0, ic == 3, None))
                P.mm(groups, [(o_fm, None), wo_sb], [bank], tag=pfx + "wo")
                P.dve(lambda e, b=bank, t=t, hf=hf: e.tensor_add(
                    xint.t[:, 1024 * t + 512 * hf:1024 * t + 512 * hf + 512],
                    xint.t[:, 1024 * t + 512 * hf:1024 * t + 512 * hf + 512],
                    b.t[:, :]), [bank, (xint, ("x", t))], [(xint, ("x", t))],
                    tag=pfx + "wo_ev")
        P.free(o_fm)
        P.free(wo_sb)
        P.free(o_tm)

        _chk("wo")
        sh3 = P.dram(pfx + f"sh3_{l}", [2, T, 512], BF16, shared=True)
        send_half_bf(sh3, flag_base + 3 * l + 2, pfx + "send3")
        _chk("send3")
        P.poll_flag(flag_base + 3 * l + 2, tag=pfx + "poll3")
        _chk("poll3")
        shif2 = P.sbuf(pfx + "shif2", [128, NT * 512], BF16, side="hi")
        for hh in range(2):
            shift_recv(shif2, sh3, ("polled", flag_base + 3 * l + 2), 8 * hh, 8,
                       pfx + f"recv3{hh}", dst_col0=8 * hh * 512)

        lng_bc = P.sbuf(pfx + "lng", [128, 1024], F32, side="hi")
        lnb_bc = P.sbuf(pfx + "lnb", [128, 1024], F32, side="hi")
        b2r = P.sbuf(pfx + "b2r", [1, 1024], BF16, side="hi")
        P.dma("sp", lng_bc.t[:, :], io["lng_bc"].t.ap()[:, 1024 * l:1024 * l + 1024],
              [io["lng_bc"]], [lng_bc], tag=pfx + "ldlng")
        P.dma("sp", lnb_bc.t[:, :], io["lnb_bc"].t.ap()[:, 1024 * l:1024 * l + 1024],
              [io["lnb_bc"]], [lnb_bc], tag=pfx + "ldlnb")
        P.dma("pool", b2r.t[:, :], io["b2r"].t.ap()[0:1, 1024 * l:1024 * l + 1024],
              [io["b2r"]], [b2r], tag=pfx + "ldb2")
        _chk("exch3")
        ln2_fm = P.sbuf(pfx + "ln2fm", [128, DC * T], BF16, side="hi")
        for t in range(NT):
            par = t % 2
            h2a = xint.t[:, 1024 * t:1024 * t + 512]
            h2b = shif2.t[:, 512 * t:512 * t + 512]
            bn, mv, rs = sc.bn[par], sc.mv[par], sc.rs[par]

            def f_bn(e, h2a=h2a, h2b=h2b, bn=bn):
                e.bn_stats(bn.t[:, 0:6], h2a)
                return e.bn_stats(bn.t[:, 6:12], h2b)
            P.dve(f_bn, [(xint, ("x", t)), (shif2, "d")], [bn],
                  tag=pfx + "l2bn")
            P.dve(lambda e, bn=bn, mv=mv: e.bn_aggr(mv.t[:, :], bn.t[:, 0:12]),
                  [bn], [mv], tag=pfx + "l2ag")
            P.act(rs.t[:, :], mv.t[:, 1:2], AF.Ln, [mv], [rs], bias=LNEPS,
                  tag=pfx + "l2l")
            P.act(rs.t[:, :], rs.t[:, :], AF.Exp, [rs], [rs], scale=-0.5,
                  tag=pfx + "l2e")
            stage = sc.stage[par]
            tmp = sc.tmp[par]
            P.dve(lambda e, h2a=h2a, mv=mv, rs=rs, tmp=tmp: e.tensor_scalar(
                tmp.t[:, 0:512], h2a, mv.t[:, 0:1], rs.t[:, 0:1],
                ALU.subtract, ALU.mult), [(xint, ("x", t)), mv, rs], [(tmp, "a")],
                tag=pfx + "l2a1")
            P.dve(lambda e, h2b=h2b, mv=mv, rs=rs, tmp=tmp: e.tensor_scalar(
                tmp.t[:, 512:1024], h2b, mv.t[:, 0:1], rs.t[:, 0:1],
                ALU.subtract, ALU.mult), [(shif2, "d"), mv, rs],
                [(tmp, "b")], tag=pfx + "l2a2")
            P.dve(lambda e, tmp=tmp: e.tensor_mul(
                tmp.t[:, :], tmp.t[:, :], lng_bc.t[:, :]),
                [(tmp, "a"), (tmp, "b"), lng_bc], [(tmp, "g")], tag=pfx + "l2g")
            P.dve(lambda e, tmp=tmp, s=stage: e.tensor_add(
                s.t[:, :], tmp.t[:, :], lnb_bc.t[:, :]),
                [(tmp, "g"), lnb_bc], [(stage, "ln2")], tag=pfx + "l2b")
            for g in range(2):
                bank = banks[6 + (t + g) % 2]
                prs = []
                for c in range(4):
                    prs.append((bank.t[:, 128 * c:128 * c + 128],
                                stage.t[:, 128 * (4 * g + c):128 * (4 * g + c) + 128]))
                P.tr(prs, ident, [(stage, "ln2")], [bank], tag=pfx + "l2tr")
                oap = ln2_fm.t[:, :].rearrange("p (c u) -> p c u", c=8)[
                    :, 4 * g:4 * g + 4, 128 * t:128 * t + 128]
                P.dve(lambda e, b=bank, o=oap: e.tensor_copy(o, b.t[:, 0:512]),
                      [bank], [(ln2_fm, (t, g))], tag=pfx + "l2tv")
        P.free(shif2)

        _chk("ln2")
        pg = P.sbuf(pfx + "pg", [128, NIC * 256], BF16, side="hi")
        w1s = [P.sbuf(pfx + f"w1s{i}", [128, DC * 256], BF16, side="hi")
               for i in range(2)]
        w2s = [P.sbuf(pfx + f"w2s{i}", [128, 1024], BF16, side="hi")
               for i in range(2)]
        ag_st = qk_st[0]
        for g8 in range(8):
            tok0 = 256 * g8
            t0 = tok0 // 128
            for c in range(NIC):
                wg = w1s[(c // 2) % 2]
                if c % 2 == 0:
                    P.dma("sp", wg.t[:, :].rearrange("p (d u) -> p d u", d=DC),
                          w1g_bf.t.ap()[l].rearrange("(d p) f -> p d f", p=128)[
                              :, :, 128 * c:128 * c + 256],
                          [(w1g_bf, l)], [wg], tag=pfx + "lw1g")
                bA = banks[4 + c % 2]
                bG = banks[5 - c % 2]
                groups = []
                for dc in range(DC):
                    groups.append((bA.t[:, 0:256],
                                   w1a_sb.t[:, INNER * dc + 128 * c:
                                            INNER * dc + 128 * c + 128],
                                   ln2_fm.t[:, dc * T + tok0:dc * T + tok0 + 256],
                                   dc == 0, dc == DC - 1, None))
                P.mm(groups, [(ln2_fm, None), (w1a_sb, dc)], [bA],
                     tag=pfx + "w1mm")
                groups = []
                for dc in range(DC):
                    groups.append((bG.t[:, 0:256],
                                   wg.t[:, 256 * dc + 128 * (c % 2):
                                        256 * dc + 128 * (c % 2) + 128],
                                   ln2_fm.t[:, dc * T + tok0:dc * T + tok0 + 256],
                                   dc == 0, dc == DC - 1, None))
                P.mm(groups, [(ln2_fm, None), wg], [bG], tag=pfx + "w1mm")
                P.dve(lambda e, b=bA, c=c, l=l, s=ag_st: e.tensor_scalar(
                    s.t[:, 0:256], b.t[:, 0:256],
                    b1aT.t[:, NIC * l + c:NIC * l + c + 1], None, ALU.add),
                    [bA, b1aT], [(ag_st, "a")], tag=pfx + "aev")
                P.act(ag_st.t[:, 256:512], bG.t[:, 0:256], AF.Gelu,
                      [bG, b1gT], [(ag_st, "g")],
                      bias=b1gT.t[:, NIC * l + c:NIC * l + c + 1], tag=pfx + "gev")
                P.dve(lambda e, s=ag_st, c=c: e.tensor_mul(
                    pg.t[:, 256 * c:256 * c + 256], s.t[:, 0:256],
                    s.t[:, 256:512]), [(ag_st, "a"), (ag_st, "g")],
                    [(pg, c)], tag=pfx + "glu")
            for c in range(NIC):
                w2b = w2s[c % 2]
                P.dma("act", w2b.t[:, :],
                      w2_bf.t.ap()[l][128 * c:128 * c + 128, :],
                      [(w2_bf, l)], [w2b], tag=pfx + "lw2")
                for tt in range(2):
                    for hf in range(2):
                        bank = banks[2 * tt + hf]
                        P.mm([(bank.t[:, :],
                               pg.t[:, 256 * c + 128 * tt:256 * c + 128 * tt + 128],
                               w2b.t[:, 512 * hf:512 * hf + 512],
                               c == 0, False, None)],
                             [(pg, c), w2b], [bank], tag=pfx + "w2mm")
            for tt in range(2):
                for hf in range(2):
                    bank = banks[2 * tt + hf]
                    P.mm([(bank.t[:, :], onesr.t[0:1, :],
                           b2r.t[0:1, 512 * hf:512 * hf + 512],
                           False, True, None)],
                         [onesr, b2r], [bank], tag=pfx + "b2mm")
                    t_abs = t0 + tt
                    P.dve(lambda e, b=bank, t=t_abs, hf=hf: e.tensor_add(
                        xint.t[:, 1024 * t + 512 * hf:1024 * t + 512 * hf + 512],
                        xint.t[:, 1024 * t + 512 * hf:1024 * t + 512 * hf + 512],
                        b.t[:, :]), [bank, (xint, ("x", t_abs))],
                        [(xint, ("x", t_abs))], tag=pfx + "y2ev")
        P.free(pg)
        P.free(w1a_sb)
        for b in w1s + w2s:
            P.free(b)
        P.free(ln2_fm)
        P.free(lng_bc)
        P.free(lnb_bc)
        P.free(b2r)

    try:
        for l in range(2):
            if stop_after == "ln":
                raise _Stop()
            _layer(l)
            if stop_after.startswith("layer0"):
                raise _Stop()
    except _Stop:
        pass

    if "xint" in dump:
        _dump(P, nc, xint, "xint")
    xstage = P.sbuf(pfx + "xstage", [128, 2 * 1024], F32, side="hi")
    for t in range(NT):
        par = t % 2
        P.dma("sp", xstage.t[:, 1024 * par:1024 * par + 1024],
              xspill.t.ap()[:, 1024 * t:1024 * t + 1024], [xspill],
              [(xstage, par)], tag=pfx + "xrld")
        ra = xint.t[:, 1024 * t:1024 * t + 1024]
        rkeys = [(xint, ("x", t))]
        emit_ln_stats(P, sc, par, ra, rkeys)
        tmp = sc.tmp[par]
        emit_ln_apply(P, sc, par, ra, tmp.t[:, :], rkeys, [(tmp, "r")])
        P.dve(lambda e, t=t, m=tmp, par=par: e.tensor_add(
            xint.t[:, 1024 * t:1024 * t + 1024],
            xstage.t[:, 1024 * par:1024 * par + 1024], m.t[:, :]),
            [(tmp, "r"), (xstage, par)], rkeys, tag=pfx + "resid")
    P.free(xstage)
    for b in list(tabs.values()) + [qs_bc, ks_bc] + list(amasks.values()) + \
            [b1aT, b1gT, onesr, zrow]:
        P.free(b)
    for b in qk_st + l2s:
        P.free(b)
    return xint


def l0ffn_host_prep(w, parity):
    DIM_HEAD, WINDOW, n = 64, 512, 4096
    pos = (2 * np.arange(T) + parity).astype(np.float32)
    half = np.arange(0, DIM_HEAD, 2, dtype=np.float32)
    inv_freq = 1.0 / np.power(np.float32(10000.0), half / DIM_HEAD)
    fr = pos[:, None] * inv_freq[None, :]
    fr = np.concatenate([fr, fr], axis=1)
    base = (half + 0.4 * DIM_HEAD) / (1.4 * DIM_HEAD)
    power = (pos - n // 2) / float(WINDOW // 2)
    scale = np.power(base[None, :].astype(np.float32), power[:, None].astype(np.float32))
    scale = np.concatenate([scale, scale], axis=1)
    cos, sin = np.cos(fr), np.sin(fr)
    sign = np.concatenate([-np.ones(32, np.float32), np.ones(32, np.float32)])
    Aq = (cos * scale * 8.0).astype(np.float32)
    Bq = (sign[None, :] * sin * scale * 8.0).astype(np.float32)
    Ak = (cos / scale).astype(np.float32)
    Bk = (sign[None, :] * sin / scale).astype(np.float32)

    def lay(tab):
        return np.ascontiguousarray(
            tab.reshape(16, 128, 64).transpose(1, 0, 2).reshape(128, 16 * 64))

    i = np.arange(128)[:, None]
    j = np.arange(128)[None, :]
    BIG = np.float32(-1e30)
    am_od = np.where(j <= i, np.float32(0), BIG)
    am_of = np.where(j >= i, np.float32(0), BIG)
    am_pd = np.where(j <= i - 1 + parity, np.float32(0), BIG)
    am_pf = np.where(j >= i + parity, np.float32(0), BIG)

    w1 = w["L0_ffn_w1"]
    w1a = np.zeros((2, D, INNER), np.float32)
    w1g = np.zeros((2, D, INNER), np.float32)
    w1a[:, :, :2730] = w1[:, :, :2730]
    w1g[:, :, :2730] = w1[:, :, 2730:]
    w2p = np.zeros((2, INNER, D), np.float32)
    w2p[:, :2730, :] = w["L0_ffn_w2"]
    b1 = w["L0_ffn_b1"]
    b1aT = np.zeros((128, 2 * NIC), np.float32)
    b1gT = np.zeros((128, 2 * NIC), np.float32)
    b1a = np.zeros((2, INNER), np.float32)
    b1g = np.zeros((2, INNER), np.float32)
    b1a[:, :2730] = b1[:, :2730]
    b1g[:, :2730] = b1[:, 2730:]
    for l in range(2):
        b1aT[:, NIC * l:NIC * l + NIC] = b1a[l].reshape(NIC, 128).T
        b1gT[:, NIC * l:NIC * l + NIC] = b1g[l].reshape(NIC, 128).T
    b2r = np.zeros((1, 2 * 1024), np.float32)
    b2r[0, :1024] = w["L0_ffn_b2"][0]
    b2r[0, 1024:] = w["L0_ffn_b2"][1]
    qsb = np.zeros((128, 2 * 64), np.float32)
    ksb = np.zeros((128, 2 * 64), np.float32)
    for l in range(2):
        qsb[:, 64 * l:64 * l + 64] = np.broadcast_to(w["L0_ffn_q_scale"][l], (128, 64))
        ksb[:, 64 * l:64 * l + 64] = np.broadcast_to(w["L0_ffn_k_scale"][l], (128, 64))
    lngb = np.zeros((128, 2 * 1024), np.float32)
    lnbb = np.zeros((128, 2 * 1024), np.float32)
    for l in range(2):
        lngb[:, 1024 * l:1024 * l + 1024] = np.broadcast_to(w["L0_ffn_ln_g"][l], (128, 1024))
        lnbb[:, 1024 * l:1024 * l + 1024] = np.broadcast_to(w["L0_ffn_ln_b"][l], (128, 1024))
    return {
        "wqkv": np.ascontiguousarray(w["L0_ffn_wqkv"]),
        "wo": np.ascontiguousarray(w["L0_ffn_wo"]),
        "w1a": w1a, "w1g": w1g, "w2": w2p,
        "b1aT": b1aT, "b1gT": b1gT, "b2r": b2r,
        "qs_bc": qsb, "ks_bc": ksb, "lng_bc": lngb, "lnb_bc": lnbb,
        "Aq": lay(Aq), "Bq": lay(Bq), "Ak": lay(Ak), "Bk": lay(Bk),
        "am_od": am_od, "am_of": am_of, "am_pd": am_pd, "am_pf": am_pf,
    }


def build_slot1_test(dump=(), stop_after=""):
    nc = _new_nc()
    xin = nc.dram_tensor("xin", [T, D], F32, kind="ExternalInput")
    xout = nc.dram_tensor("xout", [T, D], F32, kind="ExternalOutput")
    P, banks, sc, ident = _prog_setup(nc)
    parity = P.wrap(nc.dram_tensor("parity", [1, 1], U32, kind="ExternalInput"),
                    "parity", "dram")
    nonce = P.wrap(nc.dram_tensor("nonce", [1, 8], U32, kind="ExternalInput"),
                   "nonce", "dram")
    P.setup_parity(parity, nonce, 8)
    shapes = {
        "wqkv": ([2, D, 1536], F32), "wo": ([2, 512, D], F32),
        "w1a": ([2, D, INNER], F32), "w1g": ([2, D, INNER], F32),
        "w2": ([2, INNER, D], F32),
        "b1aT": ([128, 2 * NIC], F32), "b1gT": ([128, 2 * NIC], F32),
        "b2r": ([1, 2 * 1024], F32),
        "qs_bc": ([128, 2 * 64], F32), "ks_bc": ([128, 2 * 64], F32),
        "lng_bc": ([128, 2 * 1024], F32), "lnb_bc": ([128, 2 * 1024], F32),
        "Aq": ([128, 16 * 64], F32), "Bq": ([128, 16 * 64], F32),
        "Ak": ([128, 16 * 64], F32), "Bk": ([128, 16 * 64], F32),
        "am_od": ([128, 128], F32), "am_of": ([128, 128], F32),
        "am_pd": ([128, 128], F32), "am_pf": ([128, 128], F32),
    }
    io = {k: P.wrap(nc.dram_tensor(k, s, d, kind="ExternalInput"), k, "dram")
          for k, (s, d) in shapes.items()}
    x = P.sbuf("x", [128, NT * 1024], F32)
    _load_stream(P, x, P.wrap(xin, "xin", "dram"))
    x = emit_l0ffn(P, sc, banks, ident, x, io, flag_base=0, xkey=lambda t: t,
                   dump=dump, nc=nc, stop_after=stop_after)
    _store_stream(P, x, P.wrap(xout, "xout", "dram"))
    P.emit()
    return nc


_BUILD_CACHE = {}


def _get_prog(name, fn):
    if name not in _BUILD_CACHE:
        _BUILD_CACHE[name] = fn()
    return _BUILD_CACHE[name]


def kernel_unfused(**inputs):
    w = {k: np.asarray(v) for k, v in inputs.items()}
    x_full = w["L0_attn_hidden_states"]
    rng = np.random.default_rng()
    nonces = {s: rng.integers(1, 2 ** 31, size=(1, n)).astype(np.uint32)
              for s, n in (("a0", 4), ("f", 8), ("a1", 4))}

    per_core = []
    for c in range(8):
        b, h = c // 2, c % 2
        im = {"parity": np.array([[h]], np.uint32)}
        im["x0"] = np.ascontiguousarray(x_full[b, h::2]).astype(np.float32)
        im["a0"] = dict(attn_host_prep(w["L0_attn_Wq"], w["L0_attn_bq"],
                                       w["L0_attn_Wk"], w["L0_attn_bk"],
                                       w["L0_attn_Wv"], w["L0_attn_bv"],
                                       h, False, True), nonce=nonces["a0"])
        im["f"] = dict(l0ffn_host_prep(w, h), nonce=nonces["f"])
        im["a1"] = dict(attn_host_prep(w["L1_attn_Wq"], w["L1_attn_bq"],
                                       w["L1_attn_Wk"], w["L1_attn_bk"],
                                       w["L1_attn_Wv"], w["L1_attn_bv"],
                                       h, True, False), nonce=nonces["a1"])
        im["moe"] = moe_host_prep(w)
        per_core.append(im)

    nc0 = _get_prog("s0", lambda: build_slot_attn_test(causal=False))
    maps = [dict(per_core[c]["a0"], xin=per_core[c]["x0"],
                 parity=per_core[c]["parity"]) for c in range(8)]
    r0 = run_bass_kernel_spmd(nc0, maps, core_ids=list(range(8)))
    xs = [r0.results[c]["xout"] for c in range(8)]

    nc1 = _get_prog("s1", lambda: build_slot1_test())
    maps = [dict(per_core[c]["f"], xin=xs[c], parity=per_core[c]["parity"])
            for c in range(8)]
    r1 = run_bass_kernel_spmd(nc1, maps, core_ids=list(range(8)))
    xs = [r1.results[c]["xout"] for c in range(8)]

    nc2 = _get_prog("s2", lambda: build_slot_attn_test(causal=True))
    maps = [dict(per_core[c]["a1"], xin=xs[c], parity=per_core[c]["parity"])
            for c in range(8)]
    r2 = run_bass_kernel_spmd(nc2, maps, core_ids=list(range(8)))
    xs = [r2.results[c]["xout"] for c in range(8)]

    nc3 = _get_prog("s3", lambda: build_slot3_test())
    maps = [dict(per_core[c]["moe"], xin=xs[c]) for c in range(8)]
    r3 = run_bass_kernel_spmd(nc3, maps, core_ids=list(range(8)))
    xs = [r3.results[c]["xout"] for c in range(8)]

    out = np.zeros((4, 4096, 1024), np.float32)
    for c in range(8):
        b, h = c // 2, c % 2
        out[b, h::2] = xs[c]
    return out


def build_fused():
    nc = _new_nc()
    xin = nc.dram_tensor("xin", [T, D], F32, kind="ExternalInput")
    xout = nc.dram_tensor("xout", [T, D], F32, kind="ExternalOutput")
    P, banks, sc, ident = _prog_setup(nc)
    parity = P.wrap(nc.dram_tensor("parity", [1, 1], U32, kind="ExternalInput"),
                    "parity", "dram")
    nonce = P.wrap(nc.dram_tensor("nonce", [1, 8], U32, kind="ExternalInput"),
                   "nonce", "dram")
    P.setup_parity(parity, nonce, 8)

    def dram_in(name, shape, dtype=F32):
        return P.wrap(nc.dram_tensor(name, shape, dtype, kind="ExternalInput"),
                      name, "dram")

    io_a0 = {k: dram_in("a0_" + k, s) for k, s in
             [("Wq", [D, D]), ("Wk", [D, D]), ("Wv", [D, D]),
              ("bqT", [128, 8]), ("bkT", [128, 8]), ("bv_bc", [128, 1024])]}
    io_a1 = {k: dram_in("a1_" + k, s) for k, s in
             [("Wq", [D, D]), ("Wk", [D, D]), ("Wv", [D, D]),
              ("bqT", [128, 8]), ("bkT", [128, 8]), ("bv_bc", [128, 1024])]}
    for k in ("mask_own", "mask_partner"):
        io_a1[k] = dram_in("a1_" + k, [128, 128], BF16)
    fshapes = {
        "wqkv": ([2, D, 1536], F32), "wo": ([2, 512, D], F32),
        "w1a": ([2, D, INNER], F32), "w1g": ([2, D, INNER], F32),
        "w2": ([2, INNER, D], F32),
        "b1aT": ([128, 2 * NIC], F32), "b1gT": ([128, 2 * NIC], F32),
        "b2r": ([1, 2 * 1024], F32),
        "qs_bc": ([128, 2 * 64], F32), "ks_bc": ([128, 2 * 64], F32),
        "lng_bc": ([128, 2 * 1024], F32), "lnb_bc": ([128, 2 * 1024], F32),
        "Aq": ([128, 16 * 64], F32), "Bq": ([128, 16 * 64], F32),
        "Ak": ([128, 16 * 64], F32), "Bk": ([128, 16 * 64], F32),
        "am_od": ([128, 128], F32), "am_of": ([128, 128], F32),
        "am_pd": ([128, 128], F32), "am_pf": ([128, 128], F32),
    }
    io_f = {k: dram_in("f_" + k, s, d) for k, (s, d) in fshapes.items()}
    mshapes = {
        "rW": [D, 16], "rb_bc": [128, 16], "sW1": [D, D], "sb1T": [128, 8],
        "sW2": [D, D], "sb2_bc": [128, D], "eW1": [15, D, D],
        "eb1T": [128, 15 * 8], "eW2": [15, D, D], "eb2": [15, D],
    }
    io_m = {k: dram_in("m_" + k, s) for k, s in mshapes.items()}

    xk = lambda t: ("x", t)
    x = P.sbuf("x", [128, NT * 1024], F32)
    _load_stream(P, x, P.wrap(xin, "xin", "dram"))
    precast, pieces, cast_fin = emit_ffn_casts(P, io_f, "f_")
    x = emit_attn(P, sc, banks, ident, x, io_a0, causal=False, flag_base=0,
                  xkey=xk, pfx="a0_", drip=pieces)
    for pc in pieces:
        pc()
    pieces.clear()
    cast_fin()
    x = emit_l0ffn(P, sc, banks, ident, x, io_f, flag_base=1, xkey=xk, pfx="f_",
                   precast=precast)
    x = emit_attn(P, sc, banks, ident, x, io_a1, causal=True, flag_base=7,
                  xkey=xk, pfx="a1_")
    x = emit_moe(P, sc, banks, ident, x, io_m, xkey=xk)
    _store_stream(P, x, P.wrap(xout, "xout", "dram"))
    P.emit()
    return nc


def kernel_fused(**inputs):
    w = {k: np.asarray(v) for k, v in inputs.items()}
    x_full = w["L0_attn_hidden_states"]
    rng = np.random.default_rng()
    nonce = rng.integers(1, 2 ** 31, size=(1, 8)).astype(np.uint32)
    nc = _get_prog("fused", build_fused)
    maps = []
    for c in range(8):
        b, h = c // 2, c % 2
        im = {"parity": np.array([[h]], np.uint32), "nonce": nonce,
              "xin": np.ascontiguousarray(x_full[b, h::2]).astype(np.float32)}
        im.update({"a0_" + k: v for k, v in attn_host_prep(
            w["L0_attn_Wq"], w["L0_attn_bq"], w["L0_attn_Wk"], w["L0_attn_bk"],
            w["L0_attn_Wv"], w["L0_attn_bv"], h, False, True).items()})
        im.update({"f_" + k: v for k, v in l0ffn_host_prep(w, h).items()})
        im.update({"a1_" + k: v for k, v in attn_host_prep(
            w["L1_attn_Wq"], w["L1_attn_bq"], w["L1_attn_Wk"], w["L1_attn_bk"],
            w["L1_attn_Wv"], w["L1_attn_bv"], h, True, False).items()})
        im.update({"m_" + k: v for k, v in moe_host_prep(w).items()})
        maps.append(im)
    res = run_bass_kernel_spmd(nc, maps, core_ids=list(range(8)))
    out = np.zeros((4, 4096, 1024), np.float32)
    for c in range(8):
        b, h = c // 2, c % 2
        out[b, h::2] = res.results[c]["xout"]
    return out


kernel = kernel_fused
